# version 24
# baseline (speedup 1.0000x reference)
import numpy as np
import ml_dtypes
import concourse.bass as bass
import concourse.mybir as mybir
from concourse.bass_utils import run_bass_kernel_spmd
from contextlib import ExitStack

F32 = mybir.dt.float32
BF16 = mybir.dt.bfloat16
I32 = mybir.dt.int32
AF = mybir.ActivationFunctionType
ALU = mybir.AluOpType
AX = mybir.AxisListType

D = 1024
DH = 4096
DA = 2048

PI_F = float(np.float32(np.pi))
TWO_PI_F = 2.0 * PI_F
INV2PI = float(np.float32(1.0 / (2.0 * np.pi)))
_c1 = np.float32(6.28125)
_c2 = np.float32(2.0 * np.pi - float(_c1))
_c3 = np.float32(2.0 * np.pi - float(_c1) - float(_c2))
CW1 = float(_c1)
CW2 = float(np.float32(float(_c2) + float(_c3)))
SQRT5 = float(np.float32(np.sqrt(5.0)))
EPS_ACC = float(np.float32(1e-8) / np.float32(5.0))


class CFG:
    def __init__(self, T=2048, M=512, XC=128, MIDC=256, NMOV=512):
        self.T, self.M, self.XC, self.MIDC, self.NMOV = T, M, XC, MIDC, NMOV
        assert T % M == 0 and M % XC == 0 and M % MIDC == 0 and M % NMOV == 0
        assert 4 * XC == M
        self.NP = T // M
        self.NCH = M // NMOV
        assert self.NCH == 1
        assert 12 * MIDC + NMOV + 4 * XC == 8 * M


class Buf:
    __slots__ = ("w", "r", "name")

    def __init__(self, name=""):
        self.w = {}
        self.r = {}
        self.name = name


def _acc(d, tok):
    s, v = tok
    if d.get(s, 0) < v:
        d[s] = v


def build_program(cfg: CFG):
    T, M, XC, MIDC, NMOV = cfg.T, cfg.M, cfg.XC, cfg.MIDC, cfg.NMOV
    NP = cfg.NP
    nc = bass.Bass("TRN2", target_bir_lowering=False)

    dr = {}
    dr["x_fm"] = nc.dram_tensor("x_fm", [T // XC, 128 * 8 * XC], F32,
                                kind="ExternalInput")
    dr["xp_fm"] = nc.dram_tensor("xp_fm", [T // XC, 128 * 8 * XC], F32,
                                 kind="ExternalInput")
    dr["pos_o"] = nc.dram_tensor("pos_o", [128, T], F32, kind="ExternalInput")
    dr["pos_p"] = nc.dram_tensor("pos_p", [128, T], F32, kind="ExternalInput")
    dr["mask"] = nc.dram_tensor("mask", [128, 1], F32, kind="ExternalInput")
    for nm, shp in [("wmag_t", [D, D]), ("wq_t", [D, D]), ("w1_t", [DA, DH]),
                    ("w2_t", [D, DA]), ("fw1_t", [DH, D]), ("fw2_t", [DH, D]),
                    ("h1_t", [DH, DH]), ("h2_t", [DH, DH]), ("w3_t", [D, DH])]:
        dr[nm] = nc.dram_tensor(nm, shp, F32, kind="ExternalInput")
    dr["h1b"] = nc.dram_tensor("h1b", [DH, DH], BF16, kind="Internal")
    dr["h2b"] = nc.dram_tensor("h2b", [DH, DH], BF16, kind="Internal")
    VEC_SHAPES = [("v_bmag", 8), ("v_bq", 8), ("v_b2", 8), ("v_rp", 8),
                  ("v_s3", 8), ("v_b1", 16), ("v_lng", 32), ("v_lnb", 32),
                  ("v_s1", 32), ("v_s2", 32), ("v_h1s", 32), ("v_h2s", 32)]
    for nm, w in VEC_SHAPES:
        dr[nm] = nc.dram_tensor(nm, [128, w], F32, kind="ExternalInput")
    dr["ones2"] = nc.dram_tensor("ones2", [33, 128], F32, kind="ExternalInput")
    dr["sel_a"] = nc.dram_tensor("sel_a", [128, 33], BF16, kind="ExternalInput")
    dr["sel_b"] = nc.dram_tensor("sel_b", [128, 33], BF16, kind="ExternalInput")
    dr["y_fm"] = nc.dram_tensor("y_fm", [T // XC, 128 * 8 * XC], F32,
                                kind="ExternalOutput")

    es = ExitStack()
    sbt = lambda name, shp, dt: es.enter_context(nc.sbuf_tensor(name, shp, dt))

    big1 = sbt("big1", [128, 64 * M], BF16)
    big2 = sbt("big2", [128, 64 * M], BF16)
    obuf = sbt("obuf", [128, 8 * M], F32)
    xn_hi = sbt("xn_hi", [128, 8 * M], BF16)
    xo = sbt("xo", [128, 16 * XC + M], F32)
    invt = sbt("invt", [128, 8 * M], BF16)
    wstage = sbt("wstage", [128, 8192], BF16)
    lnbmu = sbt("lnbmu", [128, T], BF16)
    lnbr = sbt("lnbr", [128, T], F32)
    carries = sbt("carries", [128, 24], F32)
    epsc = sbt("epsc", [128, 2], F32)
    vec = {nm: sbt("sv" + nm, [128, w], F32) for nm, w in VEC_SHAPES}
    scf = sbt("scf", [128, 48], F32)
    mask_sb = sbt("mask_sb", [128, 1], F32)
    ones2 = sbt("ones2_sb", [33, 128], F32)
    sel_a = sbt("sel_a_sb", [128, 33], BF16)
    sel_b = sbt("sel_b_sb", [128, 33], BF16)

    big1f = big1.bitcast(F32)
    big2f = big2.bitcast(F32)
    big2i = big2.bitcast(I32)

    psum = [es.enter_context(nc.psum_tensor(f"ps{i}", [128, 512], F32))
            for i in range(8)]

    sems = {nm: es.enter_context(nc.semaphore(nm))
            for nm in ["LDV", "LDS0", "LDS1", "LDP", "LDW0", "LDW1", "LDW2",
                       "LDW3", "STS0", "STS1", "VS", "SS", "TS", "CVT", "CV2"]}

    B1REG = [Buf(f"b1_{i}") for i in range(64)]
    big1_regs = [(i * 2 * M, (i + 1) * 2 * M, B1REG[i]) for i in range(64)]

    B2REG = {}
    big2_regs = []
    B_U = {}
    B_SQ, B_XB = {}, {}
    for q in range(2):
        ab = q * 64 * M
        ab = q * 2 * (32 * M)
        for i in range(16):
            b = Buf(f"b2_{q}_{i}")
            B2REG[(q, i)] = b
            big2_regs.append((ab + i * 2 * M, ab + (i + 1) * 2 * M, b))
        ub = ab + 2 * 16 * M
        for u in range(12):
            b = Buf(f"u{q}_{u}")
            B_U[(q, u)] = b
            big2_regs.append((ub + u * 4 * MIDC, ub + (u + 1) * 4 * MIDC, b))
        sq_b = ub + 12 * 4 * MIDC
        B_SQ[q] = Buf(f"sq{q}")
        big2_regs.append((sq_b, sq_b + 4 * NMOV, B_SQ[q]))
        xb_b = sq_b + 4 * NMOV
        B_XB[q] = Buf(f"xb{q}")
        big2_regs.append((xb_b, xb_b + 2 * 4 * XC * 2, B_XB[q]))
    assert sum(z - a for (a, z, _) in big2_regs) == 2 * 64 * M

    B_OB = [Buf(f"ob{i}") for i in range(4)]
    obuf_regs = [(i * 8 * M, (i + 1) * 8 * M, B_OB[i]) for i in range(4)]

    def regs_in(regs, lo, hi):
        return [b for (a, z, b) in regs if a < hi and lo < z]

    def b1r(lo_bf, hi_bf):
        return regs_in(big1_regs, lo_bf * 2, hi_bf * 2)

    def b2r_abs(lo_bf, hi_bf):
        return regs_in(big2_regs, lo_bf * 2, hi_bf * 2)

    def obr(lo_f32, hi_f32):
        return regs_in(obuf_regs, lo_f32 * 4, hi_f32 * 4)

    B_XN = Buf("xn")
    B_WS4 = [Buf(f"ws4_{i}") for i in range(4)]
    B_WS = [B_WS4[0:2], B_WS4[2:4]]
    B_LNM, B_LNR = Buf("lnbmu"), Buf("lnbr")
    B_XST = [Buf("xst0"), Buf("xst1")]
    B_POS = Buf("pos")
    B_INV = [Buf(f"inv{i}") for i in range(8)]
    B_CR, B_CI, B_CM = Buf("cr"), Buf("ci"), Buf("cm")
    B_PS = [Buf(f"ps{i}") for i in range(8)]
    B_VEC = {nm: Buf(nm) for nm, _ in VEC_SHAPES}
    B_MASK, B_EPS = Buf("mask"), Buf("eps")
    B_SCF = Buf("scf")
    B_CVH1 = Buf("cvh1")
    B_CVH2 = Buf("cvh2")
    B_O2, B_SELA, B_SELB = Buf("o2"), Buf("sela"), Buf("selb")

    ENG = ("sync", "gpsimd", "vector", "scalar", "tensor")
    q_ = {e: [] for e in ENG}
    cnt = {"VS": 0, "SS": 0, "TS": 0, "LDV": 0, "LDS0": 0, "LDS1": 0,
           "LDP": 0, "LDW0": 0, "LDW1": 0, "LDW2": 0, "LDW3": 0,
           "STS0": 0, "STS1": 0, "CVT": 0, "CV2": 0}
    hw = {e: {} for e in ENG}
    CHAIN = {"vector": "VS", "scalar": "SS", "tensor": "TS"}

    def _mkwaits(eng, deps):
        out = []
        for s, v in deps:
            if hw[eng].get(s, 0) >= v:
                continue
            hw[eng][s] = v
            out.append((s, v))
        return out

    def _deps(reads, writes, extra):
        deps = list(extra)
        for b in reads:
            deps += b.w.items()
        for b in writes:
            deps += b.w.items()
            deps += b.r.items()
        return deps

    def op(eng, fn, reads=(), writes=(), extra=()):
        sem = CHAIN[eng]
        waits = _mkwaits(eng, _deps(reads, writes, extra))
        cnt[sem] += 1
        tok = (sem, cnt[sem])

        def cl(e, fn=fn, waits=waits, sem=sem):
            for s, v in waits:
                e.wait_ge(sems[s], v)
            fn(e).then_inc(sems[sem], 1)
        q_[eng].append(cl)
        for b in reads:
            _acc(b.r, tok)
        for b in writes:
            _acc(b.w, tok)
        return tok

    def dop(queue, sem, out_ap, in_ap, reads=(), writes=(), extra=()):
        waits = _mkwaits(queue, _deps(reads, writes, extra))
        cnt[sem] += 16
        tok = (sem, cnt[sem])

        def cl(e, out_ap=out_ap, in_ap=in_ap, waits=waits, sem=sem):
            for s, v in waits:
                e.wait_ge(sems[s], v)
            e.dma_start(out=out_ap, in_=in_ap).then_inc(sems[sem], 16)
        q_[queue].append(cl)
        for b in reads:
            _acc(b.r, tok)
        for b in writes:
            _acc(b.w, tok)
        return tok

    V = lambda fn, reads=(), writes=(), extra=(): op("vector", fn, reads, writes, extra)
    S = lambda fn, reads=(), writes=(), extra=(): op("scalar", fn, reads, writes, extra)
    TE = lambda fn, reads=(), writes=(), extra=(): op("tensor", fn, reads, writes, extra)

    def ap3(tensor, off, dims):
        return bass.AP(tensor, off, [list(d) for d in dims])

    def a1b(q):
        return (q % 2) * 32 * M

    def a2b(q):
        return (q % 2) * 32 * M

    def a2f(q):
        return (q % 2) * 16 * M

    def uf(q, u, n=None):
        n = MIDC if n is None else n
        base = a2f(q) + 8 * M + u * MIDC
        return big2f[:, base:base + n]

    def uf_bf(q, u, n=None):
        n = MIDC if n is None else n
        base = 2 * (a2f(q) + 8 * M + u * MIDC)
        return big2[:, base:base + n]

    def uf_i32(q, n=None):
        n = MIDC if n is None else n
        base = a2f(q) + 8 * M + 11 * MIDC
        return big2i[:, base:base + n]

    def ubuf(q, u):
        return B_U[(q % 2, u)]

    def sq_off(q, slot):
        return 2 * (a2f(q) + 8 * M + 12 * MIDC) + slot * NMOV

    def xbf_off(q):
        return 2 * (a2f(q) + 8 * M + 12 * MIDC) + 2 * NMOV

    XST0, XST1, POS = 0, 8 * XC, 16 * XC

    def xst_ap(buf, n_tok):
        return ap3(xo, buf * 8 * XC, [[16 * XC + M, 128], [n_tok, 8], [1, n_tok]])

    def xst_blk(buf, b, n_tok):
        return ap3(xo, buf * 8 * XC + b * n_tok, [[16 * XC + M, 128], [1, n_tok]])

    def xdram_ap(xdram, off, n_tok):
        assert n_tok == XC and off % XC == 0
        c = off // XC
        return ap3(xdram, c * (128 * 8 * XC), [[8 * XC, 128], [1, 8 * XC]])

    def wslice_ap(wdram, m, IN, OUT, ngroups=1):
        return ap3(wdram, m * 128 * IN, [[IN, 128], [128 * IN, ngroups], [1, IN]])

    def wstage_ap(buf, K):
        return ap3(wstage, buf * 4096, [[8192, 128], [128, K], [1, 128]])

    V(lambda e: e.memset(carries[:], 0.0), writes=[B_CR, B_CI, B_CM])
    V(lambda e: e.memset(epsc[:, 0:1], 1e-5), writes=[B_EPS])
    V(lambda e: e.memset(epsc[:, 1:2], PI_F / 2), writes=[B_EPS])
    _init_bufs = [B_MASK, B_O2, B_SELA, B_SELB] + list(B_VEC.values())
    for nm, _w in VEC_SHAPES:
        dop("sync", "LDV", vec[nm][:], dr[nm].ap(), writes=[B_VEC[nm]])
    dop("sync", "LDV", mask_sb[:], dr["mask"].ap(), writes=[B_MASK])
    dop("sync", "LDV", ones2[:], dr["ones2"].ap(), writes=[B_O2])
    dop("sync", "LDV", sel_a[:], dr["sel_a"].ap(), writes=[B_SELA])
    dop("sync", "LDV", sel_b[:], dr["sel_b"].ap(), writes=[B_SELB])
    for _b in _init_bufs:
        _b.w = {"LDV": cnt["LDV"]}

    def stat_pipeline(csize, dim, sum_reads, mk_sums, mu_out=None, r_out=None,
                      v_out=None, sq=0, par=0):
        base = a2f(sq) + 8 * M
        if csize <= MIDC:
            offs = {k: s * 512 + par * MIDC for k, s in
                    [("mean", 0), ("var", 1), ("srow", 4), ("ln", 5)]}
            bufs = {k: [ubuf(sq, 2 * (v // 512) + par)]
                    for k, v in offs.items()}
            bc1_lo = par * 128
            bc2_bank, bc2_lo = 7, 256 + par * 128
        else:
            offs = {"mean": 0, "var": 512, "srow": 2048, "ln": 2560}
            bufs = {k: sfb(sq, v // 512) for k, v in offs.items()}
            bc1_lo = 0
            bc2_bank, bc2_lo = 3, 0

        def A(k, n=csize):
            return big2f[:, base + offs[k]:base + offs[k] + n]
        sbase = base + offs["srow"]
        if mk_sums is not None:
            TE(mk_sums, reads=list(sum_reads) + [B_SELA, B_SELB],
               writes=[B_PS[3]])
        V(lambda e: e.tensor_copy(
            out=big2f[0:33, sbase:sbase + csize], in_=psum[3][0:33, 0:csize]),
          reads=[B_PS[3]], writes=bufs["srow"])
        TE(lambda e: e.matmul(psum[7][:, bc1_lo:bc1_lo + csize], ones2[0:1, :],
                              big2f[0:1, sbase:sbase + csize],
                              start=True, stop=True),
           reads=bufs["srow"] + [B_O2], writes=[B_PS[7]])
        V(lambda e: e.tensor_scalar(out=A("mean"),
                                    in0=psum[7][:, bc1_lo:bc1_lo + csize],
                                    scalar1=1.0 / dim, scalar2=None,
                                    op0=ALU.mult),
          reads=[B_PS[7]], writes=bufs["mean"])
        TE(lambda e: e.matmul(psum[bc2_bank][:, bc2_lo:bc2_lo + csize],
                              ones2[32:33, :],
                              big2f[32:33, sbase:sbase + csize],
                              start=True, stop=True),
           reads=bufs["srow"] + [B_O2], writes=[B_PS[bc2_bank]])
        V(lambda e: e.tensor_tensor(out=A("var"), in0=A("mean"), in1=A("mean"),
                                    op=ALU.mult),
          reads=bufs["mean"], writes=bufs["var"])
        V(lambda e: e.tensor_scalar(out=A("ln"),
                                    in0=psum[bc2_bank][:, bc2_lo:bc2_lo + csize],
                                    scalar1=1.0 / dim, scalar2=None,
                                    op0=ALU.mult),
          reads=[B_PS[bc2_bank]], writes=bufs["ln"])
        V(lambda e: e.tensor_tensor(out=A("var"), in0=A("ln"), in1=A("var"),
                                    op=ALU.subtract),
          reads=bufs["ln"] + bufs["var"], writes=bufs["var"])
        S(lambda e: e.activation(out=A("ln"), in_=A("var"), func=AF.Ln,
                                 bias=epsc[:, 0:1], scale=1.0),
          reads=bufs["var"] + [B_EPS], writes=bufs["ln"])
        r_ap, r_bufs = r_out
        S(lambda e: e.activation(out=r_ap, in_=A("ln"), func=AF.Exp, scale=-0.5),
          reads=bufs["ln"], writes=r_bufs)
        if mu_out is not None:
            ap_, bufs_ = mu_out
            V(lambda e, ap_=ap_: e.tensor_copy(out=ap_, in_=A("mean")),
              reads=bufs["mean"], writes=bufs_)
        if v_out is not None:
            ap_, bufs_ = v_out
            V(lambda e, ap_=ap_: e.tensor_tensor(out=ap_, in0=A("mean"),
                                                 in1=r_ap, op=ALU.mult),
              reads=bufs["mean"] + list(r_bufs), writes=bufs_)

    def stats_chunk(xdram, off, csize, buf, mu_out, r_out):
        ldx = dop("sync", f"LDS{buf}", xst_ap(buf, csize),
                  xdram_ap(xdram, off, csize), writes=[B_XST[buf]])
        if buf == 0:
            xbf0 = xbf_off(0)
            xb_bufs = [B_XB[0]]
            sqb = 2 * (a2f(0) + 8 * M + 2 * 512)
            sq_bufs = [ubuf(0, 4), ubuf(0, 5)]
        else:
            xbf0 = sq_off(0, 0)
            xb_bufs = [B_SQ[0]]
            sqb = 2 * (a2f(0) + 8 * M + 3 * 512)
            sq_bufs = [ubuf(0, 6), ubuf(0, 7)]
        V(lambda e, xbf0=xbf0, buf=buf: e.tensor_copy(
            out=big2[:, xbf0:xbf0 + 8 * csize], in_=xst_ap(buf, csize)),
          reads=[B_XST[buf]], writes=xb_bufs, extra=[ldx])
        S(lambda e, sqb=sqb, buf=buf: e.activation(
            out=big2[:, sqb:sqb + 8 * csize], in_=xst_ap(buf, csize),
            func=AF.Square),
          reads=[B_XST[buf]], writes=sq_bufs, extra=[ldx])

        def mk_sums(e, xbf0=xbf0, sqb=sqb):
            last = None
            for b in range(8):
                last = e.matmul(psum[3][0:33, 0:csize], sel_a[:],
                                big2[:, xbf0 + b * csize:xbf0 + (b + 1) * csize],
                                start=(b == 0), stop=False)
            for b in range(8):
                last = e.matmul(psum[3][0:33, 0:csize], sel_b[:],
                                big2[:, sqb + b * csize:sqb + (b + 1) * csize],
                                start=False, stop=(b == 7))
            return last
        stat_pipeline(csize, D, xb_bufs + sq_bufs, mk_sums,
                      mu_out=mu_out, r_out=r_out, sq=0, par=buf)

    def sf(q, i, n=M):
        base = a2f(q) + 8 * M + i * 512
        return big2f[:, base:base + n]

    def si32(q, i, n=M):
        base = a2f(q) + 8 * M + i * 512
        return big2i[:, base:base + n]

    def sfb(q, i):
        return [ubuf(q, 2 * i), ubuf(q, 2 * i + 1)]

    def tl(i, n=M):
        return invt[:, (1 + i) * M:(1 + i) * M + n]

    def tlb(i):
        return B_INV[1 + i]

    def emit_sincos(q, P, A, B, C, sin_t, cos_t, tf=None, tfb=None):
        tf = tf or tl
        tfb = tfb or tlb
        V(lambda e: e.tensor_scalar(out=sf(q, A), in0=sf(q, P),
                                    scalar1=INV2PI, scalar2=0.5,
                                    op0=ALU.mult, op1=ALU.add),
          reads=sfb(q, P), writes=sfb(q, A))
        V(lambda e: e.tensor_copy(out=si32(q, C), in_=sf(q, A)),
          reads=sfb(q, A), writes=sfb(q, C))
        V(lambda e: e.tensor_copy(out=sf(q, A), in_=si32(q, C)),
          reads=sfb(q, C), writes=sfb(q, A))
        V(lambda e: e.scalar_tensor_tensor(out=sf(q, C), in0=sf(q, A),
                                           scalar=-CW1, in1=sf(q, P),
                                           op0=ALU.mult, op1=ALU.add),
          reads=sfb(q, A) + sfb(q, P), writes=sfb(q, C))
        V(lambda e: e.scalar_tensor_tensor(out=sf(q, B), in0=sf(q, A),
                                           scalar=-CW2, in1=sf(q, C),
                                           op0=ALU.mult, op1=ALU.add),
          reads=sfb(q, A) + sfb(q, C), writes=sfb(q, B))
        V(lambda e: e.tensor_scalar(out=sf(q, C), in0=sf(q, B), scalar1=-PI_F,
                                    scalar2=None, op0=ALU.is_lt),
          reads=sfb(q, B), writes=sfb(q, C))
        V(lambda e: e.scalar_tensor_tensor(out=sf(q, A), in0=sf(q, C),
                                           scalar=TWO_PI_F, in1=sf(q, B),
                                           op0=ALU.mult, op1=ALU.add),
          reads=sfb(q, C) + sfb(q, B), writes=sfb(q, A))
        S(lambda e: e.activation(out=tf(sin_t), in_=sf(q, A), func=AF.Sin),
          reads=sfb(q, A), writes=[tfb(sin_t)])
        V(lambda e: e.tensor_scalar(out=sf(q, C), in0=sf(q, A), scalar1=PI_F / 2,
                                    scalar2=None, op0=ALU.is_gt),
          reads=sfb(q, A), writes=sfb(q, C))
        V(lambda e: e.scalar_tensor_tensor(out=sf(q, B), in0=sf(q, C),
                                           scalar=-TWO_PI_F, in1=sf(q, A),
                                           op0=ALU.mult, op1=ALU.add),
          reads=sfb(q, C) + sfb(q, A), writes=sfb(q, B))
        S(lambda e: e.activation(out=tf(cos_t), in_=sf(q, B), func=AF.Sin,
                                 bias=epsc[:, 1:2], scale=1.0),
          reads=sfb(q, B) + [B_EPS], writes=[tfb(cos_t)])

    def xn_build(xdram, dram_base, mu_ap_fn, r_ap_fn, mu_bufs, r_bufs,
                 dst_fn=None, dst_bufs_fn=None):
        if dst_fn is None:
            dst_fn = lambda b, c: xn_hi[:, b * M + c * XC:b * M + (c + 1) * XC]
            dst_bufs_fn = lambda b, c: [B_XN]
        for c in range(M // XC):
            buf = c % 2
            ldx = dop("sync", f"LDS{buf}", xst_ap(buf, XC),
                      xdram_ap(xdram, dram_base + c * XC, XC), writes=[B_XST[buf]])
            for b in range(8):
                V(lambda e, b=b, buf=buf, c=c: e.tensor_tensor(
                    out=xst_blk(buf, b, XC), in0=xst_blk(buf, b, XC),
                    in1=mu_ap_fn(c), op=ALU.subtract),
                  reads=[B_XST[buf]] + mu_bufs, writes=[B_XST[buf]], extra=[ldx])
                V(lambda e, b=b, c=c, buf=buf: e.tensor_tensor(
                    out=dst_fn(b, c), in0=xst_blk(buf, b, XC),
                    in1=r_ap_fn(c), op=ALU.mult),
                  reads=[B_XST[buf]] + r_bufs, writes=dst_bufs_fn(b, c))

    BANKS6 = (0, 1, 2, 4, 5, 6)

    def emit_convert(dst, src, IN, lo, hi, done_buf, sem):
        for m in range(lo, hi):
            dop("gpsimd", sem,
                ap3(dst, m * 128 * IN, [[1, 1], [1, 128 * IN]]),
                ap3(src, m * 128 * IN, [[1, 1], [1, 128 * IN]]))
        done_buf.w = {sem: cnt[sem]}

    def stream_projection(wdram, IN, OUT, mov_fn, mov_bufs_fn, evac_fn,
                          extra=(), hook=None, reads_fn=None):
        K = IN // 128
        for m in range(OUT // 128):
            sbuf = m % 2
            dop("gpsimd", f"LDW{2 * sbuf}", wstage_ap(sbuf, K),
                wslice_ap(wdram, m, IN, OUT), writes=B_WS[sbuf],
                reads=(reads_fn(m) if reads_fn else ()))
            bank = BANKS6[m % 6]

            def mm(e, sbuf=sbuf, K=K, bank=bank):
                last = None
                for k in range(K):
                    wt_ap = wstage[:, sbuf * 4096 + k * 128:sbuf * 4096 + (k + 1) * 128]
                    last = e.matmul(psum[bank][:, 0:NMOV], wt_ap, mov_fn(k),
                                    start=(k == 0), stop=(k == K - 1))
                return last
            mov_bufs = []
            for k in range(K):
                for bb in mov_bufs_fn(k):
                    if bb not in mov_bufs:
                        mov_bufs.append(bb)
            TE(mm, reads=B_WS[sbuf] + mov_bufs, writes=[B_PS[bank]], extra=extra)
            evac_fn(m, bank)
            if hook is not None:
                hook()

    for c in range(T // XC):
        stats_chunk(dr["x_fm"], c * XC, XC, c % 2,
                    (lnbmu[:, c * XC:(c + 1) * XC], [B_LNM]),
                    (lnbr[:, c * XC:(c + 1) * XC], [B_LNR]))

    def pmu_ap(pp, c):
        b0 = a2f(1) + 8 * M + (2 * (pp % 2)) * 512
        return big2f[:, b0 + c * XC:b0 + (c + 1) * XC]

    def pr_ap(pp, c):
        b0 = a2f(1) + 8 * M + (2 * (pp % 2) + 1) * 512
        return big2f[:, b0 + c * XC:b0 + (c + 1) * XC]

    def pmu_bufs(pp):
        return sfb(1, 2 * (pp % 2))

    def pr_bufs(pp):
        return sfb(1, 2 * (pp % 2) + 1)

    def xnp_blk(pp, b):
        arena = pp // 2
        blk = 16 + 8 * (pp % 2) + b
        return a1b(arena) + blk * M

    def mov_xnp_fn(pp):
        def mov(k):
            base = xnp_blk(pp, k)
            return big1[:, base:base + M]
        return mov

    def mov_xnp_bufs_fn(pp):
        def bufs(k):
            arena = pp // 2
            return [B1REG[32 * arena + 16 + 8 * (pp % 2) + k]]
        return bufs

    def sigp_base(pp):
        return a2b(pp // 2) + (pp % 2) * 8 * M

    def emit_prev_stats(pp):
        for c in range(M // XC):
            buf = c % 2
            stats_chunk(dr["xp_fm"], pp * M + c * XC, XC, buf,
                        (pmu_ap(pp, c), pmu_bufs(pp)),
                        (pr_ap(pp, c), pr_bufs(pp)))
            for b in range(8):
                dst = big1[:, xnp_blk(pp, b) + c * XC:
                           xnp_blk(pp, b) + (c + 1) * XC]
                dbuf = mov_xnp_bufs_fn(pp)(b)
                V(lambda e, b=b, buf=buf, c=c, pp=pp: e.tensor_tensor(
                    out=xst_blk(buf, b, XC), in0=xst_blk(buf, b, XC),
                    in1=pmu_ap(pp, c), op=ALU.subtract),
                  reads=[B_XST[buf]] + pmu_bufs(pp), writes=[B_XST[buf]])
                V(lambda e, b=b, buf=buf, c=c, pp=pp, dst=dst: e.tensor_tensor(
                    out=dst, in0=xst_blk(buf, b, XC), in1=pr_ap(pp, c),
                    op=ALU.mult),
                  reads=[B_XST[buf]] + pr_bufs(pp), writes=dbuf)

    for pp in range(NP):
        emit_prev_stats(pp)

    def emit_wmag_prev(pp, hook=None):
        base = sigp_base(pp)

        def evac_sigp(m, bank, base=base):
            S(lambda e, m=m, bank=bank, base=base: e.activation(
                out=big2[:, base + m * M:base + (m + 1) * M],
                in_=psum[bank][:, 0:NMOV], func=AF.Sigmoid,
                bias=vec["v_bmag"][:, m:m + 1], scale=1.0),
              reads=[B_PS[bank], B_VEC["v_bmag"]],
              writes=b2r_abs(base + m * M, base + (m + 1) * M))
        stream_projection(dr["wmag_t"], D, D, mov_xnp_fn(pp),
                          mov_xnp_bufs_fn(pp), evac_sigp, hook=hook)

    def tlB(i, n=M):
        return big1[:, a1b(1) + i * M:a1b(1) + i * M + n]

    def tlbB(i):
        return B1REG[32 + i]

    def tlB0(i, n=M):
        return big1[:, i * M:i * M + n]

    def tlbB0(i):
        return B1REG[i]

    def sc_store(b, j, n=M):
        base = a1b(1) + (2 * b + j) * M
        return big1[:, base:base + n]

    def sc_store_b(b, j):
        return B1REG[32 + 2 * b + j]

    def emit_rot_scalars():
        def s8(i):
            return sf(0, i, 8)

        def s8b(i):
            return sfb(0, i)

        def s8i(i):
            return si32(0, i, 8)
        for jx in range(3):
            V(lambda e, jx=jx: e.tensor_scalar(
                out=s8(2), in0=vec["v_rp"][:], scalar1=0.01 * M * (jx + 1),
                scalar2=None, op0=ALU.mult),
              reads=[B_VEC["v_rp"]], writes=s8b(2))
            V(lambda e: e.tensor_scalar(out=s8(3), in0=s8(2), scalar1=INV2PI,
                                        scalar2=0.5, op0=ALU.mult, op1=ALU.add),
              reads=s8b(2), writes=s8b(3))
            V(lambda e: e.tensor_copy(out=s8i(5), in_=s8(3)),
              reads=s8b(3), writes=s8b(5))
            V(lambda e: e.tensor_copy(out=s8(3), in_=s8i(5)),
              reads=s8b(5), writes=s8b(3))
            V(lambda e: e.scalar_tensor_tensor(out=s8(5), in0=s8(3),
                                               scalar=-CW1, in1=s8(2),
                                               op0=ALU.mult, op1=ALU.add),
              reads=s8b(3) + s8b(2), writes=s8b(5))
            V(lambda e: e.scalar_tensor_tensor(out=s8(4), in0=s8(3),
                                               scalar=-CW2, in1=s8(5),
                                               op0=ALU.mult, op1=ALU.add),
              reads=s8b(3) + s8b(5), writes=s8b(4))
            V(lambda e: e.tensor_scalar(out=s8(5), in0=s8(4), scalar1=-PI_F,
                                        scalar2=None, op0=ALU.is_lt),
              reads=s8b(4), writes=s8b(5))
            V(lambda e: e.scalar_tensor_tensor(out=s8(3), in0=s8(5),
                                               scalar=TWO_PI_F, in1=s8(4),
                                               op0=ALU.mult, op1=ALU.add),
              reads=s8b(5) + s8b(4), writes=s8b(3))
            S(lambda e, jx=jx: e.activation(
                out=scf[:, 24 + 8 * jx:24 + 8 * jx + 8], in_=s8(3),
                func=AF.Sin),
              reads=s8b(3), writes=[B_SCF])
            V(lambda e: e.tensor_scalar(out=s8(5), in0=s8(3), scalar1=PI_F / 2,
                                        scalar2=None, op0=ALU.is_gt),
              reads=s8b(3), writes=s8b(5))
            V(lambda e: e.scalar_tensor_tensor(out=s8(4), in0=s8(5),
                                               scalar=-TWO_PI_F, in1=s8(3),
                                               op0=ALU.mult, op1=ALU.add),
              reads=s8b(5) + s8b(3), writes=s8b(4))
            S(lambda e, jx=jx: e.activation(
                out=scf[:, 8 * jx:8 * jx + 8], in_=s8(4), func=AF.Sin,
                bias=epsc[:, 1:2], scale=1.0),
              reads=s8b(4) + [B_EPS], writes=[B_SCF])

    def emit_lite(pp):
        uq = pp // 2
        base = sigp_base(pp)
        dop("sync", "LDP", xo[:, POS:POS + M],
            ap3(dr["pos_p"], pp * M, [[T, 128], [1, M]]), writes=[B_POS])

        def chain(b):
            if b % 2 == 0:
                return uq, tl, tlb
            return 1 - uq, tlB0, tlbB0

        def seg_a(b):
            sq, tf, tfb = chain(b)
            if pp == 0:
                V(lambda e, b=b, sq=sq: e.tensor_scalar(
                    out=sf(sq, 2), in0=xo[:, POS:POS + M],
                    scalar1=vec["v_rp"][:, b:b + 1],
                    scalar2=0.01, op0=ALU.mult, op1=ALU.mult),
                  reads=[B_POS, B_VEC["v_rp"]], writes=sfb(sq, 2))
                emit_sincos(sq, 2, 3, 4, 5, 0, 1,
                            tf=lambda i, b=b: sc_store(b, i),
                            tfb=lambda i, b=b: sc_store_b(b, i))
            else:
                jx = pp - 1
                cc = scf[:, 8 * jx + b:8 * jx + b + 1]
                sc = scf[:, 24 + 8 * jx + b:24 + 8 * jx + b + 1]
                V(lambda e, b=b, cc=cc: e.tensor_scalar(
                    out=tf(4), in0=sc_store(b, 0), scalar1=cc, scalar2=None,
                    op0=ALU.mult),
                  reads=[sc_store_b(b, 0), B_SCF], writes=[tfb(4)])
                V(lambda e, b=b, sc=sc: e.scalar_tensor_tensor(
                    out=tf(0), in0=sc_store(b, 1), scalar=sc, in1=tf(4),
                    op0=ALU.mult, op1=ALU.add),
                  reads=[sc_store_b(b, 1), B_SCF, tfb(4)], writes=[tfb(0)])
                V(lambda e, b=b, sc=sc: e.tensor_scalar(
                    out=tf(4), in0=sc_store(b, 0), scalar1=sc, scalar2=None,
                    op0=ALU.mult),
                  reads=[sc_store_b(b, 0), B_SCF, tfb(4)], writes=[tfb(4)])
                V(lambda e, b=b, cc=cc: e.scalar_tensor_tensor(
                    out=tf(1), in0=sc_store(b, 1), scalar=cc, in1=tf(4),
                    op0=ALU.mult, op1=ALU.subtract),
                  reads=[sc_store_b(b, 1), B_SCF, tfb(4)], writes=[tfb(1)])

        def seg_b(b):
            sq, tf, tfb = chain(b)
            if pp == 0:
                sin_ap, sin_b = sc_store(b, 0), sc_store_b(b, 0)
                cos_ap, cos_b = sc_store(b, 1), sc_store_b(b, 1)
            else:
                sin_ap, sin_b = tf(0), tfb(0)
                cos_ap, cos_b = tf(1), tfb(1)
            gsl = slice(base + b * M, base + (b + 1) * M)
            sreg = b2r_abs(gsl.start, gsl.stop)
            xnpb = xnp_blk(pp, b)
            xnp = big1[:, xnpb:xnpb + M]
            xnp_buf = mov_xnp_bufs_fn(pp)(b)
            V(lambda e, gsl=gsl, xnp=xnp: e.tensor_tensor(
                out=tf(2), in0=big2[:, gsl], in1=xnp, op=ALU.mult),
              reads=sreg + xnp_buf, writes=[tfb(2)])
            V(lambda e, cos_ap=cos_ap: e.tensor_tensor(
                out=tf(3), in0=tf(2), in1=cos_ap, op=ALU.mult),
              reads=[tfb(2), cos_b], writes=[tfb(3)])
            V(lambda e, sq=sq: e.tensor_scalar(
                out=tf(4), in0=tf(3), scalar1=1.0, scalar2=None,
                op0=ALU.mult, op1=ALU.add, accum_out=sf(sq, 3, 1)),
              reads=[tfb(3)], writes=[tfb(4)] + sfb(sq, 3))
            V(lambda e, b=b, sq=sq: e.tensor_tensor(
                out=carries[:, b:b + 1], in0=carries[:, b:b + 1],
                in1=sf(sq, 3, 1), op=ALU.add),
              reads=sfb(sq, 3) + [B_CR], writes=[B_CR])
            V(lambda e, sin_ap=sin_ap: e.tensor_tensor(
                out=tf(3), in0=tf(2), in1=sin_ap, op=ALU.mult),
              reads=[tfb(2), sin_b], writes=[tfb(3)])
            V(lambda e, sq=sq: e.tensor_scalar(
                out=tf(4), in0=tf(3), scalar1=1.0, scalar2=None,
                op0=ALU.mult, op1=ALU.add, accum_out=sf(sq, 3, 1)),
              reads=[tfb(3)], writes=[tfb(4)] + sfb(sq, 3))
            V(lambda e, b=b, sq=sq: e.tensor_tensor(
                out=carries[:, 8 + b:9 + b], in0=carries[:, 8 + b:9 + b],
                in1=sf(sq, 3, 1), op=ALU.add),
              reads=sfb(sq, 3) + [B_CI], writes=[B_CI])
            V(lambda e, gsl=gsl, sq=sq: e.tensor_scalar(
                out=tf(4), in0=big2[:, gsl], scalar1=1.0, scalar2=None,
                op0=ALU.mult, op1=ALU.add, accum_out=sf(sq, 3, 1)),
              reads=sreg + [tfb(4)], writes=[tfb(4)] + sfb(sq, 3))
            V(lambda e, b=b, sq=sq: e.tensor_tensor(
                out=carries[:, 16 + b:17 + b], in0=carries[:, 16 + b:17 + b],
                in1=sf(sq, 3, 1), op=ALU.add),
              reads=sfb(sq, 3) + [B_CM], writes=[B_CM])

        for bp in range(4):
            seg_a(2 * bp)
            seg_a(2 * bp + 1)
            seg_b(2 * bp)
            seg_b(2 * bp + 1)

    def emit_A0(pm):
        pbase = pm * M
        xn_build(dr["x_fm"], pbase,
                 lambda c, pbase=pbase: lnbmu[:, pbase + c * XC:pbase + (c + 1) * XC],
                 lambda c, pbase=pbase: lnbr[:, pbase + c * XC:pbase + (c + 1) * XC],
                 [B_LNM], [B_LNR])

    _h1ch = []
    for _m in range(32):
        def _c(_m=_m):
            dop("gpsimd", "CVT",
                ap3(dr["h1b"], _m * 128 * DH, [[1, 1], [1, 128 * DH]]),
                ap3(dr["h1_t"], _m * 128 * DH, [[1, 1], [1, 128 * DH]]))
        _h1ch.append(_c)
    _wg = [0]

    def _wmag_hook():
        _wg[0] += 1
        if _wg[0] % 3 == 0 and _h1ch:
            _h1ch.pop(0)()
    for pp in range(NP):
        emit_wmag_prev(pp, hook=_wmag_hook)
    while _h1ch:
        _h1ch.pop(0)()
    B_CVH1.w = {"CVT": cnt["CVT"]}
    emit_A0(0)
    emit_rot_scalars()
    for pp in range(NP):
        emit_lite(pp)

    V(lambda e: e.tensor_scalar(out=carries[:, 0:24], in0=carries[:, 0:24],
                                scalar1=mask_sb[:, 0:1], scalar2=None, op0=ALU.mult),
      reads=[B_MASK, B_CR, B_CI, B_CM], writes=[B_CR, B_CI, B_CM])
    V(lambda e: e.tensor_scalar(out=carries[:, 16:24], in0=carries[:, 16:24],
                                scalar1=EPS_ACC, scalar2=None, op0=ALU.add),
      reads=[B_CM], writes=[B_CM])

    def mov_xn(k):
        return xn_hi[:, k * M:(k + 1) * M]

    def emit_A1(pm):
        q = pm % 2
        KD = D // 128
        for proj, wdram_, obase, evbias, evfunc in [
                (0, dr["wmag_t"], a2b(q), "v_bmag", AF.Sigmoid),
                (1, dr["wq_t"], a2b(q) + 8 * M, "v_bq", AF.Identity)]:
            for g4 in range(KD // 4):
                h = g4 % 2
                ap_w = ap3(wstage, h * 4096, [[8192, 128], [1, 4096]])
                dop("gpsimd", f"LDW{2 * h}", ap_w,
                    wslice_ap(wdram_, 4 * g4, D, D, ngroups=4),
                    writes=B_WS[h])
                for mi in range(4):
                    m = 4 * g4 + mi
                    bank = (m % 3) if proj == 0 else 4 + (m % 3)

                    def mm1(e, h=h, mi=mi, bank=bank, KD=KD):
                        last = None
                        for k in range(KD):
                            wt_ap = wstage[:, h * 4096 + mi * 1024 + k * 128:
                                           h * 4096 + mi * 1024 + (k + 1) * 128]
                            last = e.matmul(psum[bank][:, 0:NMOV], wt_ap,
                                            mov_xn(k), start=(k == 0),
                                            stop=(k == KD - 1))
                        return last
                    TE(mm1, reads=B_WS[h] + [B_XN], writes=[B_PS[bank]])
                    S(lambda e, m=m, bank=bank, obase=obase, evbias=evbias,
                      evfunc=evfunc: e.activation(
                        out=big2[:, obase + m * M:obase + (m + 1) * M],
                        in_=psum[bank][:, 0:NMOV], func=evfunc,
                        bias=vec[evbias][:, m:m + 1], scale=1.0),
                      reads=[B_PS[bank], B_VEC[evbias]],
                      writes=b2r_abs(obase + m * M, obase + (m + 1) * M))

    def emit_mid(pm, alt=False):
        q = pm % 2
        sigb = a2b(q)
        qofb = a2b(q) + 8 * M
        ctxb = a1b(q)
        per_b = []

        def seg_pos(pm=pm):
            dop("sync", "LDP", xo[:, POS:POS + M],
                ap3(dr["pos_o"], pm * M, [[T, 128], [1, M]]), writes=[B_POS])
        for b in range(8):
            if alt and b % 2 == 1:
                sq = 1 - q
                tf, tfb = tlB, tlbB
                inv_ap, inv_rb = tlB(7), tlbB(7)
            else:
                sq = q
                tf, tfb = tl, tlb
                inv_ap, inv_rb = invt[:, 0:M], B_INV[0]
            bsl = slice(b * M, (b + 1) * M)
            gs = slice(sigb + bsl.start, sigb + bsl.stop)
            gq = slice(qofb + bsl.start, qofb + bsl.stop)
            sreg = b2r_abs(gs.start, gs.stop)
            qreg = b2r_abs(gq.start, gq.stop)
            xcr = b1r(ctxb + bsl.start, ctxb + bsl.stop)
            xsr = b1r(ctxb + 8 * M + bsl.start, ctxb + 8 * M + bsl.stop)
            rrr = b1r(ctxb + 16 * M + bsl.start, ctxb + 16 * M + bsl.stop)
            rir = b1r(ctxb + 24 * M + bsl.start, ctxb + 24 * M + bsl.stop)

            def seg1(b=b, sq=sq, gs=gs, sreg=sreg, inv_ap=inv_ap,
                     inv_rb=inv_rb):
                V(lambda e, gs=gs, b=b, sq=sq: e.tensor_tensor_scan(
                    out=sf(sq, 0), data0=big2[:, gs], data1=big2[:, gs],
                    initial=carries[:, 16 + b:17 + b], op0=ALU.add,
                    op1=ALU.bypass),
                  reads=sreg + [B_CM], writes=sfb(sq, 0))
                V(lambda e, b=b, sq=sq: e.tensor_copy(
                    out=carries[:, 16 + b:17 + b], in_=sf(sq, 0)[:, M - 1:M]),
                  reads=sfb(sq, 0), writes=[B_CM])
                S(lambda e, sq=sq: e.activation(out=sf(sq, 1), in_=sf(sq, 0),
                                                func=AF.Ln),
                  reads=sfb(sq, 0), writes=sfb(sq, 1))
                S(lambda e, sq=sq, inv_ap=inv_ap: e.activation(
                    out=inv_ap, in_=sf(sq, 1), func=AF.Exp, scale=-0.5),
                  reads=sfb(sq, 1), writes=[inv_rb])

            def seg2(b=b, sq=sq, tf=tf, tfb=tfb):
                V(lambda e, b=b, sq=sq: e.tensor_scalar(
                    out=sf(sq, 2), in0=xo[:, POS:POS + M],
                    scalar1=vec["v_rp"][:, b:b + 1],
                    scalar2=0.01, op0=ALU.mult, op1=ALU.mult),
                  reads=[B_POS, B_VEC["v_rp"]], writes=sfb(sq, 2))
                emit_sincos(sq, 2, 3, 4, 5, 0, 1, tf=tf, tfb=tfb)

            def seg3(sq=sq, gq=gq, qreg=qreg, tf=tf, tfb=tfb):
                V(lambda e, gq=gq, sq=sq: e.tensor_tensor(
                    out=sf(sq, 0), in0=sf(sq, 2), in1=big2[:, gq], op=ALU.add),
                  reads=sfb(sq, 2) + qreg, writes=sfb(sq, 0))
                emit_sincos(sq, 0, 3, 4, 5, 2, 3, tf=tf, tfb=tfb)

            def seg4(gs=gs, bsl=bsl, sreg=sreg, ctxb=ctxb, xcr=xcr, xsr=xsr,
                     tf=tf, tfb=tfb):
                V(lambda e, gs=gs, bsl=bsl: e.tensor_tensor(
                    out=tf(4), in0=big2[:, gs], in1=xn_hi[:, bsl], op=ALU.mult),
                  reads=sreg + [B_XN], writes=[tfb(4)])
                V(lambda e, bsl=bsl, ctxb=ctxb: e.tensor_tensor(
                    out=big1[:, ctxb + bsl.start:ctxb + bsl.stop],
                    in0=xn_hi[:, bsl], in1=tf(1), op=ALU.mult),
                  reads=[B_XN, tfb(1)], writes=xcr)
                V(lambda e, bsl=bsl, ctxb=ctxb: e.tensor_tensor(
                    out=big1[:, ctxb + 8 * M + bsl.start:
                              ctxb + 8 * M + bsl.stop],
                    in0=xn_hi[:, bsl], in1=tf(0), op=ALU.mult),
                  reads=[B_XN, tfb(0)], writes=xsr)
                V(lambda e: e.tensor_tensor(out=tf(5), in0=tf(4), in1=tf(1),
                                            op=ALU.mult),
                  reads=[tfb(4), tfb(1)], writes=[tfb(5)])
                V(lambda e: e.tensor_tensor(out=tf(6), in0=tf(4), in1=tf(0),
                                            op=ALU.mult),
                  reads=[tfb(4), tfb(0)], writes=[tfb(6)])

            def seg5(b=b, sq=sq, tf=tf, tfb=tfb):
                V(lambda e, b=b, sq=sq: e.tensor_tensor_scan(
                    out=sf(sq, 3), data0=tf(5), data1=tf(5),
                    initial=carries[:, b:b + 1], op0=ALU.add, op1=ALU.bypass),
                  reads=[tfb(5), B_CR], writes=sfb(sq, 3))
                V(lambda e, b=b, sq=sq: e.tensor_copy(
                    out=carries[:, b:b + 1], in_=sf(sq, 3)[:, M - 1:M]),
                  reads=sfb(sq, 3), writes=[B_CR])
                V(lambda e, b=b, sq=sq: e.tensor_tensor_scan(
                    out=sf(sq, 4), data0=tf(6), data1=tf(6),
                    initial=carries[:, 8 + b:9 + b], op0=ALU.add,
                    op1=ALU.bypass),
                  reads=[tfb(6), B_CI], writes=sfb(sq, 4))
                V(lambda e, b=b, sq=sq: e.tensor_copy(
                    out=carries[:, 8 + b:9 + b], in_=sf(sq, 4)[:, M - 1:M]),
                  reads=sfb(sq, 4), writes=[B_CI])

            def seg6(sq=sq, bsl=bsl, ctxb=ctxb, rrr=rrr, rir=rir, tf=tf,
                     tfb=tfb, inv_ap=inv_ap, inv_rb=inv_rb):
                V(lambda e, sq=sq, inv_ap=inv_ap: e.scalar_tensor_tensor(
                    out=tf(4), in0=sf(sq, 3), scalar=SQRT5, in1=inv_ap,
                    op0=ALU.mult, op1=ALU.mult),
                  reads=sfb(sq, 3) + [inv_rb], writes=[tfb(4)])
                V(lambda e, sq=sq, inv_ap=inv_ap: e.scalar_tensor_tensor(
                    out=tf(5), in0=sf(sq, 4), scalar=SQRT5, in1=inv_ap,
                    op0=ALU.mult, op1=ALU.mult),
                  reads=sfb(sq, 4) + [inv_rb], writes=[tfb(5)])
                V(lambda e: e.tensor_tensor(out=tf(6), in0=tf(4), in1=tf(3),
                                            op=ALU.mult),
                  reads=[tfb(4), tfb(3)], writes=[tfb(6)])
                V(lambda e: e.tensor_tensor(out=tf(0), in0=tf(5), in1=tf(2),
                                            op=ALU.mult),
                  reads=[tfb(5), tfb(2)], writes=[tfb(0)])
                V(lambda e, bsl=bsl, ctxb=ctxb: e.tensor_tensor(
                    out=big1[:, ctxb + 16 * M + bsl.start:
                              ctxb + 16 * M + bsl.stop],
                    in0=tf(6), in1=tf(0), op=ALU.add),
                  reads=[tfb(6), tfb(0)], writes=rrr)
                V(lambda e: e.tensor_tensor(out=tf(1), in0=tf(5), in1=tf(3),
                                            op=ALU.mult),
                  reads=[tfb(5), tfb(3)], writes=[tfb(1)])
                V(lambda e: e.tensor_tensor(out=tf(2), in0=tf(4), in1=tf(2),
                                            op=ALU.mult),
                  reads=[tfb(4), tfb(2)], writes=[tfb(2)])
                V(lambda e, bsl=bsl, ctxb=ctxb: e.tensor_tensor(
                    out=big1[:, ctxb + 24 * M + bsl.start:
                              ctxb + 24 * M + bsl.stop],
                    in0=tf(1), in1=tf(2), op=ALU.subtract),
                  reads=[tfb(1), tfb(2)], writes=rir)
            per_b.append([seg1, seg2, seg3, seg4, seg5, seg6])
        segs = [seg_pos]
        if alt:
            for bp in range(4):
                for si in range(6):
                    segs.append(per_b[2 * bp][si])
                    segs.append(per_b[2 * bp + 1][si])
        else:
            for b in range(8):
                segs += per_b[b]
        return segs

    def emit_ln2(pm):
        q = pm % 2
        ctxb = a1b(q)
        nacc = 0
        sqi = 0
        B_SQS = [Buf(f"sqs{pm}_0"), Buf(f"sqs{pm}_1")]
        for b in range(8):
            for g in range(4):
                cb = g * 8 + b
                gsb = ctxb + cb * M
                breg = b1r(gsb, gsb + M)
                first = nacc == 0
                last = nacc == 62
                nacc += 2
                TE(lambda e, gsb=gsb, first=first: e.matmul(
                    psum[3][0:33, 0:NMOV], sel_a[:], big1[:, gsb:gsb + M],
                    start=first, stop=False),
                   reads=breg + [B_SELA], writes=[B_PS[3]])
                slot = sqi % 2
                sqi += 1
                S(lambda e, gsb=gsb, slot=slot, q=q: e.activation(
                    out=big2[:, sq_off(q, slot):sq_off(q, slot) + NMOV],
                    in_=big1[:, gsb:gsb + M], func=AF.Square),
                  reads=breg, writes=[B_SQS[slot], B_SQ[q]])
                TE(lambda e, slot=slot, last=last, q=q: e.matmul(
                    psum[3][0:33, 0:NMOV], sel_b[:],
                    big2[:, sq_off(q, slot):sq_off(q, slot) + NMOV],
                    start=False, stop=last),
                   reads=[B_SQS[slot], B_SELB, B_SQ[q]], writes=[B_PS[3]])
        r2_ap = big2f[:, a2f(q) + 8 * M + 4 * MIDC:a2f(q) + 8 * M + 4 * MIDC + M]
        v_ap = big2f[:, a2f(q) + 8 * M + 6 * MIDC:a2f(q) + 8 * M + 6 * MIDC + M]
        stat_pipeline(M, DH, [], None,
                      r_out=(r2_ap, [ubuf(q, 4), ubuf(q, 5)]),
                      v_out=(v_ap, [ubuf(q, 6), ubuf(q, 7)]), sq=q)
        S(lambda e, q=q, r2_ap=r2_ap: e.activation(
            out=big2[:, sq_off(q, 0):sq_off(q, 0) + M], in_=r2_ap, func=AF.Copy),
          reads=[ubuf(q, 4), ubuf(q, 5)], writes=[B_SQ[q]])

    def emit_a3b_block(pm, b):
        q = pm % 2
        ctxb = a1b(q)
        gsb = ctxb + b * M
        breg = b1r(gsb, gsb + M)
        V(lambda e, gsb=gsb, q=q: e.tensor_tensor(
            out=big1[:, gsb:gsb + M], in0=big1[:, gsb:gsb + M],
            in1=big2[:, sq_off(q, 0):sq_off(q, 0) + M], op=ALU.mult),
          reads=breg + [B_SQ[q]], writes=breg)
        v_ap = big2f[:, a2f(q) + 8 * M + 6 * MIDC:a2f(q) + 8 * M + 6 * MIDC + M]
        V(lambda e, gsb=gsb, q=q, v_ap=v_ap: e.tensor_tensor(
            out=uf_bf(q, 0, M), in0=big1[:, gsb:gsb + M],
            in1=v_ap, op=ALU.subtract),
          reads=breg + [ubuf(q, 6), ubuf(q, 7)], writes=[ubuf(q, 0)])
        V(lambda e, b=b, gsb=gsb, q=q: e.tensor_scalar(
            out=big1[:, gsb:gsb + M], in0=uf_bf(q, 0, M),
            scalar1=vec["v_lng"][:, b:b + 1], scalar2=vec["v_lnb"][:, b:b + 1],
            op0=ALU.mult, op1=ALU.add),
          reads=[ubuf(q, 0), B_VEC["v_lng"], B_VEC["v_lnb"]], writes=breg)

    emit_A1(0)
    emit_convert(dr["h2b"], dr["h2_t"], DH, 0, 32, B_CVH2, "CV2")
    for _s in emit_mid(0, alt=True):
        _s()
    emit_ln2(0)
    for _b in range(32):
        emit_a3b_block(0, _b)

    for pm in range(NP):
        q = pm % 2
        ctxb = a1b(q)
        yb = a2b(q)

        def mov_ctx(k, ctxb=ctxb):
            return big1[:, ctxb + k * M:ctxb + (k + 1) * M]

        def mov_ctx_bufs(k, ctxb=ctxb):
            return b1r(ctxb + k * M, ctxb + (k + 1) * M)

        def evac_a1(m, bank, yb=yb):
            S(lambda e, m=m, bank=bank, yb=yb: e.activation(
                out=big2[:, yb + m * M:yb + (m + 1) * M],
                in_=psum[bank][:, 0:NMOV], func=AF.Gelu,
                bias=vec["v_b1"][:, m:m + 1], scale=1.0),
              reads=[B_PS[bank], B_VEC["v_b1"]],
              writes=b2r_abs(yb + m * M, yb + (m + 1) * M))
        stream_projection(dr["w1_t"], DH, DA, mov_ctx, mov_ctx_bufs, evac_a1)

        def mov_a1(k, yb=yb):
            return big2[:, yb + k * M:yb + (k + 1) * M]

        def mov_a1_bufs(k, yb=yb):
            return b2r_abs(yb + k * M, yb + (k + 1) * M)

        def evac_o(m, bank):
            S(lambda e, m=m, bank=bank: e.activation(
                out=obuf[:, m * M:(m + 1) * M],
                in_=psum[bank][:, 0:NMOV], func=AF.Identity,
                bias=vec["v_b2"][:, m:m + 1], scale=1.0),
              reads=[B_PS[bank], B_VEC["v_b2"]],
              writes=obr(m * M, (m + 1) * M))
        stream_projection(dr["w2_t"], DA, D, mov_a1, mov_a1_bufs, evac_o)

        KD = D // 128
        for m in range(DH // 128):
            p2 = m % 2
            slotA, slotB = p2, 2 + p2
            bankA = m % 3
            bankB = 4 + m % 3
            apA = ap3(wstage, slotA * 2048, [[8192, 128], [1, 1024]])
            apB = ap3(wstage, slotB * 2048, [[8192, 128], [1, 1024]])
            dop("gpsimd", f"LDW{slotA}", apA, wslice_ap(dr["fw1_t"], m, D, DH),
                writes=[B_WS4[slotA]])
            dop("gpsimd", f"LDW{slotB}", apB, wslice_ap(dr["fw2_t"], m, D, DH),
                writes=[B_WS4[slotB]])

            def mmA(e, slotA=slotA, bankA=bankA):
                last = None
                for k in range(KD):
                    wt_ap = wstage[:, slotA * 2048 + k * 128:slotA * 2048 + (k + 1) * 128]
                    last = e.matmul(psum[bankA][:, 0:NMOV], wt_ap, mov_xn(k),
                                    start=(k == 0), stop=(k == KD - 1))
                return last
            TE(mmA, reads=[B_WS4[slotA], B_XN], writes=[B_PS[bankA]])

            def mmB(e, slotB=slotB, bankB=bankB):
                last = None
                for k in range(KD):
                    wt_ap = wstage[:, slotB * 2048 + k * 128:slotB * 2048 + (k + 1) * 128]
                    last = e.matmul(psum[bankB][:, 0:NMOV], wt_ap, mov_xn(k),
                                    start=(k == 0), stop=(k == KD - 1))
                return last
            TE(mmB, reads=[B_WS4[slotB], B_XN], writes=[B_PS[bankB]])
            silu_reg = b1r(ctxb, ctxb + M)
            S(lambda e, m=m, bankA=bankA, ctxb=ctxb: e.activation(
                out=big1f[:, (ctxb // 2):(ctxb // 2) + M],
                in_=psum[bankA][:, 0:NMOV], func=AF.Silu,
                scale=vec["v_s1"][:, m:m + 1]),
              reads=[B_PS[bankA], B_VEC["v_s1"]], writes=silu_reg)
            V(lambda e, m=m, bankB=bankB, yb=yb, ctxb=ctxb: e.scalar_tensor_tensor(
                out=big2[:, yb + m * M:yb + (m + 1) * M],
                in0=psum[bankB][:, 0:NMOV], scalar=vec["v_s2"][:, m:m + 1],
                in1=big1f[:, (ctxb // 2):(ctxb // 2) + M],
                op0=ALU.mult, op1=ALU.mult),
              reads=[B_PS[bankB], B_VEC["v_s2"]] + silu_reg,
              writes=b2r_abs(yb + m * M, yb + (m + 1) * M))

        if pm + 1 < NP:
            emit_A0(pm + 1)
            emit_A1(pm + 1)
            pending = emit_mid(pm + 1)
        else:
            pending = []

        def hook(pending=pending):
            if pending:
                pending.pop(0)()

        def mov_f(k, yb=yb):
            return big2[:, yb + k * M:yb + (k + 1) * M]

        def mov_f_bufs(k, yb=yb):
            return b2r_abs(yb + k * M, yb + (k + 1) * M)

        def evac_g1(m, bank, ctxb=ctxb):
            S(lambda e, m=m, bank=bank, ctxb=ctxb: e.activation(
                out=big1[:, ctxb + m * M:ctxb + (m + 1) * M],
                in_=psum[bank][:, 0:NMOV], func=AF.Gelu,
                scale=vec["v_h1s"][:, m:m + 1]),
              reads=[B_PS[bank], B_VEC["v_h1s"]],
              writes=b1r(ctxb + m * M, ctxb + (m + 1) * M))
        stream_projection(dr["h1b"], DH, DH, mov_f, mov_f_bufs, evac_g1,
                          hook=hook, reads_fn=lambda m: [B_CVH1])

        def mov_g1(k, ctxb=ctxb):
            return big1[:, ctxb + k * M:ctxb + (k + 1) * M]

        def mov_g1_bufs(k, ctxb=ctxb):
            return b1r(ctxb + k * M, ctxb + (k + 1) * M)

        def evac_g2(m, bank, yb=yb):
            S(lambda e, m=m, bank=bank, yb=yb: e.activation(
                out=big2[:, yb + m * M:yb + (m + 1) * M],
                in_=psum[bank][:, 0:NMOV], func=AF.Gelu,
                scale=vec["v_h2s"][:, m:m + 1]),
              reads=[B_PS[bank], B_VEC["v_h2s"]],
              writes=b2r_abs(yb + m * M, yb + (m + 1) * M))
        stream_projection(dr["h2b"], DH, DH, mov_g1, mov_g1_bufs, evac_g2,
                          hook=hook, reads_fn=lambda m: [B_CVH2])

        while pending:
            pending.pop(0)()
        if pm + 1 < NP:
            emit_ln2(pm + 1)

        def mov_g2(k, yb=yb):
            return big2[:, yb + k * M:yb + (k + 1) * M]

        def mov_g2_bufs(k, yb=yb):
            return b2r_abs(yb + k * M, yb + (k + 1) * M)

        def evac_w3(m, bank, pm=pm):
            oreg = obr(m * M, (m + 1) * M)
            V(lambda e, m=m, bank=bank: e.scalar_tensor_tensor(
                out=obuf[:, m * M:(m + 1) * M],
                in0=psum[bank][:, 0:NMOV], scalar=vec["v_s3"][:, m:m + 1],
                in1=obuf[:, m * M:(m + 1) * M],
                op0=ALU.mult, op1=ALU.add),
              reads=[B_PS[bank], B_VEC["v_s3"]] + oreg, writes=oreg)
            if pm + 1 < NP:
                for j in range(4):
                    emit_a3b_block(pm + 1, 4 * m + j)
        stream_projection(dr["w3_t"], DH, D, mov_g2, mov_g2_bufs, evac_w3)

        pbase = pm * M

        def ob_f(buf):
            return a2f(q) + 8 * M + buf * 1024

        def ob_bufs(buf):
            return [ubuf(q, 4 * buf + j) for j in range(4)]
        for c in range(M // XC):
            buf = c % 2
            obase = ob_f(buf)
            ldx = dop("sync", f"LDS{buf}", xst_ap(buf, XC),
                      xdram_ap(dr["x_fm"], pbase + c * XC, XC), writes=[B_XST[buf]])
            for b in range(8):
                oreg = obr(b * M + c * XC, b * M + (c + 1) * XC)
                V(lambda e, b=b, buf=buf, c=c, pbase=pbase, q=q: e.tensor_tensor(
                    out=uf(q, 8, XC), in0=xst_blk(buf, b, XC),
                    in1=lnbmu[:, pbase + c * XC:pbase + (c + 1) * XC],
                    op=ALU.subtract),
                  reads=[B_XST[buf], B_LNM], writes=[ubuf(q, 8)], extra=[ldx])
                V(lambda e, c=c, pbase=pbase, q=q: e.tensor_tensor(
                    out=uf(q, 9, XC), in0=uf(q, 8, XC),
                    in1=lnbr[:, pbase + c * XC:pbase + (c + 1) * XC],
                    op=ALU.mult),
                  reads=[ubuf(q, 8), B_LNR], writes=[ubuf(q, 9)])
                V(lambda e, b=b, buf=buf, q=q: e.tensor_tensor(
                    out=uf(q, 8, XC), in0=uf(q, 9, XC), in1=xst_blk(buf, b, XC),
                    op=ALU.add),
                  reads=[ubuf(q, 9), B_XST[buf]], writes=[ubuf(q, 8)])
                V(lambda e, b=b, c=c, obase=obase, q=q: e.tensor_tensor(
                    out=big2f[:, obase + b * XC:obase + (b + 1) * XC],
                    in0=uf(q, 8, XC),
                    in1=obuf[:, b * M + c * XC:b * M + (c + 1) * XC], op=ALU.add),
                  reads=[ubuf(q, 8)] + oreg, writes=ob_bufs(buf))
            gc = (pbase + c * XC) // XC
            dop("sync", f"STS{buf}",
                ap3(dr["y_fm"], gc * (128 * 8 * XC), [[8 * XC, 128], [1, 8 * XC]]),
                ap3(big2f, ob_f(buf), [[32 * M, 128], [1, 8 * XC]]),
                reads=ob_bufs(buf))

    t0_, t1_ = cnt["STS0"], cnt["STS1"]
    q_["sync"].append(lambda e: e.wait_ge(sems["STS0"], t0_))
    q_["sync"].append(lambda e: e.wait_ge(sems["STS1"], t1_))

    block = es.enter_context(nc.Block())

    @block.sync
    def _(e):
        for f in q_["sync"]:
            f(e)

    @block.gpsimd
    def _(e):
        for f in q_["gpsimd"]:
            f(e)

    @block.vector
    def _(e):
        for f in q_["vector"]:
            f(e)

    @block.scalar
    def _(e):
        for f in q_["scalar"]:
            f(e)

    @block.tensor
    def _(e):
        for f in q_["tensor"]:
            f(e)

    es.close()
    return nc


def _layout_inputs(ins, T=2048, XC=128):
    f32 = np.float32
    x = np.asarray(ins["L0_attn_x"], dtype=f32)
    B, Sq, Dm = x.shape
    H = Sq // 2

    def pack_w(a):
        wt_ = np.asarray(a, f32).T
        IN, OUT = wt_.shape
        return np.ascontiguousarray(
            wt_.reshape(IN // 128, 128, OUT // 128, 128).transpose(2, 1, 0, 3)
        ).reshape(OUT, IN)

    wt = {
        "wmag_t": pack_w(ins["L0_attn_W_mag"]),
        "wq_t": pack_w(ins["L0_attn_W_q"]),
        "w1_t": pack_w(ins["L0_attn_W1"]),
        "w2_t": pack_w(ins["L0_attn_W2"]),
        "fw1_t": pack_w(ins["L0_ffn_w1_w"]),
        "fw2_t": pack_w(ins["L0_ffn_w2_w"]),
        "h1_t": pack_w(ins["L0_ffn_h1_w"]),
        "h2_t": pack_w(ins["L0_ffn_h2_w"]),
        "w3_t": pack_w(ins["L0_ffn_w3_w"]),
    }

    def v128(name, w):
        return np.ascontiguousarray(np.asarray(ins[name], f32).reshape(w, 128).T)

    vecs = {
        "v_bmag": v128("L0_attn_b_mag", 8), "v_bq": v128("L0_attn_b_q", 8),
        "v_b2": v128("L0_attn_b2", 8), "v_rp": v128("L0_attn_random_phases", 8),
        "v_s3": v128("L0_ffn_w3_s", 8), "v_b1": v128("L0_attn_b1", 16),
        "v_lng": v128("L0_attn_ln_g", 32), "v_lnb": v128("L0_attn_ln_b", 32),
        "v_s1": v128("L0_ffn_w1_s", 32), "v_s2": v128("L0_ffn_w2_s", 32),
        "v_h1s": v128("L0_ffn_h1_s", 32), "v_h2s": v128("L0_ffn_h2_s", 32),
    }
    ones2 = np.ones((33, 128), dtype=np.float32)
    sel_a = np.zeros((128, 33), dtype=ml_dtypes.bfloat16)
    sel_a[:, 0] = 1
    sel_b = np.zeros((128, 33), dtype=ml_dtypes.bfloat16)
    sel_b[:, 32] = 1
    pos = np.arange(Sq, dtype=f32)

    in_maps = []
    for core in range(8):
        b, h = core // 2, core % 2

        def pack_x(xt):
            return np.ascontiguousarray(
                xt.reshape(8, 128, H // XC, XC).transpose(2, 1, 0, 3)
            ).reshape(H // XC, 128 * 8 * XC)

        m = {
            "x_fm": pack_x(np.ascontiguousarray(x[b, h * H:(h + 1) * H].T)),
            "xp_fm": pack_x(np.ascontiguousarray(x[b, 0:H].T)),
            "pos_o": np.ascontiguousarray(
                np.broadcast_to(pos[h * H:(h + 1) * H], (128, H))),
            "pos_p": np.ascontiguousarray(np.broadcast_to(pos[0:H], (128, H))),
            "mask": np.full((128, 1), 1.0 if h == 1 else 0.0, dtype=f32),
            "ones2": ones2, "sel_a": sel_a, "sel_b": sel_b,
        }
        m.update(wt)
        m.update(vecs)
        in_maps.append(m)
    return in_maps


def kernel(**ins):
    in_maps = _layout_inputs(ins)
    nc = build_program(CFG())
    res = run_bass_kernel_spmd(nc, in_maps, core_ids=list(range(8)))
    x = np.asarray(ins["L0_attn_x"])
    B, Sq, Dm = x.shape
    H = Sq // 2
    out = np.empty((B, Sq, Dm), dtype=np.float32)
    XC = 128
    for core in range(8):
        b, h = core // 2, core % 2
        y4 = res.results[core]["y_fm"].reshape(H // XC, 128, 8, XC)
        y_fm = y4.transpose(2, 1, 0, 3).reshape(Dm, H)
        out[b, h * H:(h + 1) * H] = y_fm.T
    return out


# version 25
# speedup vs baseline: 1.2009x; 1.2009x over previous
import numpy as np
import ml_dtypes
import concourse.bass as bass
import concourse.mybir as mybir
from concourse.bass_utils import run_bass_kernel_spmd
from contextlib import ExitStack

F32 = mybir.dt.float32
BF16 = mybir.dt.bfloat16
I32 = mybir.dt.int32
AF = mybir.ActivationFunctionType
ALU = mybir.AluOpType
AX = mybir.AxisListType

D = 1024
DH = 4096
DA = 2048

PI_F = float(np.float32(np.pi))
TWO_PI_F = 2.0 * PI_F
INV2PI = float(np.float32(1.0 / (2.0 * np.pi)))
_c1 = np.float32(6.28125)
_c2 = np.float32(2.0 * np.pi - float(_c1))
_c3 = np.float32(2.0 * np.pi - float(_c1) - float(_c2))
CW1 = float(_c1)
CW2 = float(np.float32(float(_c2) + float(_c3)))
SQRT5 = float(np.float32(np.sqrt(5.0)))
EPS_ACC = float(np.float32(1e-8) / np.float32(5.0))


class CFG:
    def __init__(self, T=2048, M=512, XC=128, MIDC=256, NMOV=512):
        self.T, self.M, self.XC, self.MIDC, self.NMOV = T, M, XC, MIDC, NMOV
        assert T % M == 0 and M % XC == 0 and M % MIDC == 0 and M % NMOV == 0
        assert 4 * XC == M
        self.NP = T // M
        self.NCH = M // NMOV
        assert self.NCH == 1
        assert 12 * MIDC + NMOV + 4 * XC == 8 * M


class Buf:
    __slots__ = ("w", "r", "name")

    def __init__(self, name=""):
        self.w = {}
        self.r = {}
        self.name = name


def _acc(d, tok):
    s, v = tok
    if d.get(s, 0) < v:
        d[s] = v


def build_program(cfg: CFG):
    T, M, XC, MIDC, NMOV = cfg.T, cfg.M, cfg.XC, cfg.MIDC, cfg.NMOV
    NP = cfg.NP
    nc = bass.Bass("TRN2", target_bir_lowering=False)

    dr = {}
    dr["x_fm"] = nc.dram_tensor("x_fm", [T // XC, 128 * 8 * XC], F32,
                                kind="ExternalInput")
    dr["xp_fm"] = nc.dram_tensor("xp_fm", [T // XC, 128 * 8 * XC], F32,
                                 kind="ExternalInput")
    dr["pos_o"] = nc.dram_tensor("pos_o", [128, T], F32, kind="ExternalInput")
    dr["pos_p"] = nc.dram_tensor("pos_p", [128, T], F32, kind="ExternalInput")
    dr["mask"] = nc.dram_tensor("mask", [128, 1], F32, kind="ExternalInput")
    for nm, shp in [("wmag_t", [D, D]), ("wq_t", [D, D]), ("w1_t", [DA, DH]),
                    ("w2_t", [D, DA]), ("fw1_t", [DH, D]), ("fw2_t", [DH, D]),
                    ("h1_t", [DH, DH]), ("h2_t", [DH, DH]), ("w3_t", [D, DH])]:
        dr[nm] = nc.dram_tensor(nm, shp, F32, kind="ExternalInput")
    dr["h1b"] = nc.dram_tensor("h1b", [DH, DH], BF16, kind="Internal")
    dr["h2b"] = nc.dram_tensor("h2b", [DH, DH], BF16, kind="Internal")
    VEC_SHAPES = [("v_bmag", 8), ("v_bq", 8), ("v_b2", 8), ("v_rp", 8),
                  ("v_s3", 8), ("v_b1", 16), ("v_lng", 32), ("v_lnb", 32),
                  ("v_s1", 32), ("v_s2", 32), ("v_h1s", 32), ("v_h2s", 32)]
    for nm, w in VEC_SHAPES:
        dr[nm] = nc.dram_tensor(nm, [128, w], F32, kind="ExternalInput")
    dr["ones2"] = nc.dram_tensor("ones2", [33, 128], F32, kind="ExternalInput")
    dr["sel_a"] = nc.dram_tensor("sel_a", [128, 33], BF16, kind="ExternalInput")
    dr["sel_b"] = nc.dram_tensor("sel_b", [128, 33], BF16, kind="ExternalInput")
    dr["y_fm"] = nc.dram_tensor("y_fm", [T // XC, 128 * 8 * XC], F32,
                                kind="ExternalOutput")

    es = ExitStack()
    sbt = lambda name, shp, dt: es.enter_context(nc.sbuf_tensor(name, shp, dt))

    big1 = sbt("big1", [128, 64 * M], BF16)
    big2 = sbt("big2", [128, 64 * M], BF16)
    obuf = sbt("obuf", [128, 8 * M], F32)
    xn_hi = sbt("xn_hi", [128, 8 * M], BF16)
    xo = sbt("xo", [128, 16 * XC + M], F32)
    invt = sbt("invt", [128, 8 * M], BF16)
    wstage = sbt("wstage", [128, 8192], BF16)
    lnbmu = sbt("lnbmu", [128, T], BF16)
    lnbr = sbt("lnbr", [128, T], F32)
    carries = sbt("carries", [128, 24], F32)
    epsc = sbt("epsc", [128, 2], F32)
    vec = {nm: sbt("sv" + nm, [128, w], F32) for nm, w in VEC_SHAPES}
    scf = sbt("scf", [128, 48], F32)
    mask_sb = sbt("mask_sb", [128, 1], F32)
    ones2 = sbt("ones2_sb", [33, 128], F32)
    sel_a = sbt("sel_a_sb", [128, 33], BF16)
    sel_b = sbt("sel_b_sb", [128, 33], BF16)

    big1f = big1.bitcast(F32)
    big2f = big2.bitcast(F32)
    big2i = big2.bitcast(I32)

    psum = [es.enter_context(nc.psum_tensor(f"ps{i}", [128, 512], F32))
            for i in range(8)]

    sems = {nm: es.enter_context(nc.semaphore(nm))
            for nm in ["LDV", "LDS0", "LDS1", "LDP", "LDW0", "LDW1", "LDW2",
                       "LDW3", "STS0", "STS1", "VS", "SS", "TS", "CVT", "CV2"]}

    B1REG = [Buf(f"b1_{i}") for i in range(64)]
    big1_regs = [(i * 2 * M, (i + 1) * 2 * M, B1REG[i]) for i in range(64)]

    B2REG = {}
    big2_regs = []
    B_U = {}
    B_SQ, B_XB = {}, {}
    for q in range(2):
        ab = q * 64 * M
        ab = q * 2 * (32 * M)
        for i in range(16):
            b = Buf(f"b2_{q}_{i}")
            B2REG[(q, i)] = b
            big2_regs.append((ab + i * 2 * M, ab + (i + 1) * 2 * M, b))
        ub = ab + 2 * 16 * M
        for u in range(12):
            b = Buf(f"u{q}_{u}")
            B_U[(q, u)] = b
            big2_regs.append((ub + u * 4 * MIDC, ub + (u + 1) * 4 * MIDC, b))
        sq_b = ub + 12 * 4 * MIDC
        B_SQ[q] = Buf(f"sq{q}")
        big2_regs.append((sq_b, sq_b + 4 * NMOV, B_SQ[q]))
        xb_b = sq_b + 4 * NMOV
        B_XB[q] = Buf(f"xb{q}")
        big2_regs.append((xb_b, xb_b + 2 * 4 * XC * 2, B_XB[q]))
    assert sum(z - a for (a, z, _) in big2_regs) == 2 * 64 * M

    B_OB = [Buf(f"ob{i}") for i in range(4)]
    obuf_regs = [(i * 8 * M, (i + 1) * 8 * M, B_OB[i]) for i in range(4)]

    def regs_in(regs, lo, hi):
        return [b for (a, z, b) in regs if a < hi and lo < z]

    def b1r(lo_bf, hi_bf):
        return regs_in(big1_regs, lo_bf * 2, hi_bf * 2)

    def b2r_abs(lo_bf, hi_bf):
        return regs_in(big2_regs, lo_bf * 2, hi_bf * 2)

    def obr(lo_f32, hi_f32):
        return regs_in(obuf_regs, lo_f32 * 4, hi_f32 * 4)

    B_XN = Buf("xn")
    B_WS4 = [Buf(f"ws4_{i}") for i in range(4)]
    B_WS = [B_WS4[0:2], B_WS4[2:4]]
    B_LNM, B_LNR = Buf("lnbmu"), Buf("lnbr")
    B_XST = [Buf("xst0"), Buf("xst1")]
    B_POS = Buf("pos")
    B_INV = [Buf(f"inv{i}") for i in range(8)]
    B_CR, B_CI, B_CM = Buf("cr"), Buf("ci"), Buf("cm")
    B_PS = [Buf(f"ps{i}") for i in range(8)]
    B_VEC = {nm: Buf(nm) for nm, _ in VEC_SHAPES}
    B_MASK, B_EPS = Buf("mask"), Buf("eps")
    B_SCF = Buf("scf")
    B_CVH1 = Buf("cvh1")
    B_CVH2 = Buf("cvh2")
    B_O2, B_SELA, B_SELB = Buf("o2"), Buf("sela"), Buf("selb")

    ENG = ("sync", "gpsimd", "vector", "scalar", "tensor")
    q_ = {e: [] for e in ENG}
    cnt = {"VS": 0, "SS": 0, "TS": 0, "LDV": 0, "LDS0": 0, "LDS1": 0,
           "LDP": 0, "LDW0": 0, "LDW1": 0, "LDW2": 0, "LDW3": 0,
           "STS0": 0, "STS1": 0, "CVT": 0, "CV2": 0}
    hw = {e: {} for e in ENG}
    CHAIN = {"vector": "VS", "scalar": "SS", "tensor": "TS"}

    def _mkwaits(eng, deps):
        out = []
        for s, v in deps:
            if hw[eng].get(s, 0) >= v:
                continue
            hw[eng][s] = v
            out.append((s, v))
        return out

    def _deps(reads, writes, extra):
        deps = list(extra)
        for b in reads:
            deps += b.w.items()
        for b in writes:
            deps += b.w.items()
            deps += b.r.items()
        return deps

    def op(eng, fn, reads=(), writes=(), extra=()):
        sem = CHAIN[eng]
        waits = _mkwaits(eng, _deps(reads, writes, extra))
        cnt[sem] += 1
        tok = (sem, cnt[sem])

        def cl(e, fn=fn, waits=waits, sem=sem):
            for s, v in waits:
                e.wait_ge(sems[s], v)
            fn(e).then_inc(sems[sem], 1)
        q_[eng].append(cl)
        for b in reads:
            _acc(b.r, tok)
        for b in writes:
            _acc(b.w, tok)
        return tok

    def dop(queue, sem, out_ap, in_ap, reads=(), writes=(), extra=()):
        waits = _mkwaits(queue, _deps(reads, writes, extra))
        cnt[sem] += 16
        tok = (sem, cnt[sem])

        def cl(e, out_ap=out_ap, in_ap=in_ap, waits=waits, sem=sem):
            for s, v in waits:
                e.wait_ge(sems[s], v)
            e.dma_start(out=out_ap, in_=in_ap).then_inc(sems[sem], 16)
        q_[queue].append(cl)
        for b in reads:
            _acc(b.r, tok)
        for b in writes:
            _acc(b.w, tok)
        return tok

    V = lambda fn, reads=(), writes=(), extra=(): op("vector", fn, reads, writes, extra)
    S = lambda fn, reads=(), writes=(), extra=(): op("scalar", fn, reads, writes, extra)
    TE = lambda fn, reads=(), writes=(), extra=(): op("tensor", fn, reads, writes, extra)

    def ap3(tensor, off, dims):
        return bass.AP(tensor, off, [list(d) for d in dims])

    def a1b(q):
        return (q % 2) * 32 * M

    def a2b(q):
        return (q % 2) * 32 * M

    def a2f(q):
        return (q % 2) * 16 * M

    def uf(q, u, n=None):
        n = MIDC if n is None else n
        base = a2f(q) + 8 * M + u * MIDC
        return big2f[:, base:base + n]

    def uf_bf(q, u, n=None):
        n = MIDC if n is None else n
        base = 2 * (a2f(q) + 8 * M + u * MIDC)
        return big2[:, base:base + n]

    def uf_i32(q, n=None):
        n = MIDC if n is None else n
        base = a2f(q) + 8 * M + 11 * MIDC
        return big2i[:, base:base + n]

    def ubuf(q, u):
        return B_U[(q % 2, u)]

    def sq_off(q, slot):
        return 2 * (a2f(q) + 8 * M + 12 * MIDC) + slot * NMOV

    def xbf_off(q):
        return 2 * (a2f(q) + 8 * M + 12 * MIDC) + 2 * NMOV

    XST0, XST1, POS = 0, 8 * XC, 16 * XC

    def xst_ap(buf, n_tok):
        return ap3(xo, buf * 8 * XC, [[16 * XC + M, 128], [n_tok, 8], [1, n_tok]])

    def xst_blk(buf, b, n_tok):
        return ap3(xo, buf * 8 * XC + b * n_tok, [[16 * XC + M, 128], [1, n_tok]])

    def xdram_ap(xdram, off, n_tok):
        assert n_tok == XC and off % XC == 0
        c = off // XC
        return ap3(xdram, c * (128 * 8 * XC), [[8 * XC, 128], [1, 8 * XC]])

    def wslice_ap(wdram, m, IN, OUT, ngroups=1):
        return ap3(wdram, m * 128 * IN, [[IN, 128], [128 * IN, ngroups], [1, IN]])

    def wstage_ap(buf, K):
        return ap3(wstage, buf * 4096, [[8192, 128], [128, K], [1, 128]])

    V(lambda e: e.memset(carries[:], 0.0), writes=[B_CR, B_CI, B_CM])
    V(lambda e: e.memset(epsc[:, 0:1], 1e-5), writes=[B_EPS])
    V(lambda e: e.memset(epsc[:, 1:2], PI_F / 2), writes=[B_EPS])
    _init_bufs = [B_MASK, B_O2, B_SELA, B_SELB] + list(B_VEC.values())
    for nm, _w in VEC_SHAPES:
        dop("sync", "LDV", vec[nm][:], dr[nm].ap(), writes=[B_VEC[nm]])
    dop("sync", "LDV", mask_sb[:], dr["mask"].ap(), writes=[B_MASK])
    dop("sync", "LDV", ones2[:], dr["ones2"].ap(), writes=[B_O2])
    dop("sync", "LDV", sel_a[:], dr["sel_a"].ap(), writes=[B_SELA])
    dop("sync", "LDV", sel_b[:], dr["sel_b"].ap(), writes=[B_SELB])
    for _b in _init_bufs:
        _b.w = {"LDV": cnt["LDV"]}

    def stat_pipeline(csize, dim, sum_reads, mk_sums, mu_out=None, r_out=None,
                      v_out=None, sq=0, par=0):
        base = a2f(sq) + 8 * M
        if csize <= MIDC:
            offs = {k: s * 512 + par * MIDC for k, s in
                    [("mean", 0), ("var", 1), ("srow", 4), ("ln", 5)]}
            bufs = {k: [ubuf(sq, 2 * (v // 512) + par)]
                    for k, v in offs.items()}
            bc1_lo = par * 128
            bc2_bank, bc2_lo = 7, 256 + par * 128
        else:
            offs = {"mean": 0, "var": 512, "srow": 2048, "ln": 2560}
            bufs = {k: sfb(sq, v // 512) for k, v in offs.items()}
            bc1_lo = 0
            bc2_bank, bc2_lo = 3, 0

        def A(k, n=csize):
            return big2f[:, base + offs[k]:base + offs[k] + n]
        sbase = base + offs["srow"]
        if mk_sums is not None:
            TE(mk_sums, reads=list(sum_reads) + [B_SELA, B_SELB],
               writes=[B_PS[3]])
        V(lambda e: e.tensor_copy(
            out=big2f[0:33, sbase:sbase + csize], in_=psum[3][0:33, 0:csize]),
          reads=[B_PS[3]], writes=bufs["srow"])
        TE(lambda e: e.matmul(psum[7][:, bc1_lo:bc1_lo + csize], ones2[0:1, :],
                              big2f[0:1, sbase:sbase + csize],
                              start=True, stop=True),
           reads=bufs["srow"] + [B_O2], writes=[B_PS[7]])
        V(lambda e: e.tensor_scalar(out=A("mean"),
                                    in0=psum[7][:, bc1_lo:bc1_lo + csize],
                                    scalar1=1.0 / dim, scalar2=None,
                                    op0=ALU.mult),
          reads=[B_PS[7]], writes=bufs["mean"])
        TE(lambda e: e.matmul(psum[bc2_bank][:, bc2_lo:bc2_lo + csize],
                              ones2[32:33, :],
                              big2f[32:33, sbase:sbase + csize],
                              start=True, stop=True),
           reads=bufs["srow"] + [B_O2], writes=[B_PS[bc2_bank]])
        V(lambda e: e.tensor_tensor(out=A("var"), in0=A("mean"), in1=A("mean"),
                                    op=ALU.mult),
          reads=bufs["mean"], writes=bufs["var"])
        V(lambda e: e.tensor_scalar(out=A("ln"),
                                    in0=psum[bc2_bank][:, bc2_lo:bc2_lo + csize],
                                    scalar1=1.0 / dim, scalar2=None,
                                    op0=ALU.mult),
          reads=[B_PS[bc2_bank]], writes=bufs["ln"])
        V(lambda e: e.tensor_tensor(out=A("var"), in0=A("ln"), in1=A("var"),
                                    op=ALU.subtract),
          reads=bufs["ln"] + bufs["var"], writes=bufs["var"])
        S(lambda e: e.activation(out=A("ln"), in_=A("var"), func=AF.Ln,
                                 bias=epsc[:, 0:1], scale=1.0),
          reads=bufs["var"] + [B_EPS], writes=bufs["ln"])
        r_ap, r_bufs = r_out
        S(lambda e: e.activation(out=r_ap, in_=A("ln"), func=AF.Exp, scale=-0.5),
          reads=bufs["ln"], writes=r_bufs)
        if mu_out is not None:
            ap_, bufs_ = mu_out
            V(lambda e, ap_=ap_: e.tensor_copy(out=ap_, in_=A("mean")),
              reads=bufs["mean"], writes=bufs_)
        if v_out is not None:
            ap_, bufs_ = v_out
            V(lambda e, ap_=ap_: e.tensor_tensor(out=ap_, in0=A("mean"),
                                                 in1=r_ap, op=ALU.mult),
              reads=bufs["mean"] + list(r_bufs), writes=bufs_)

    def stats_chunk(xdram, off, csize, buf, mu_out, r_out):
        ldx = dop("sync", f"LDS{buf}", xst_ap(buf, csize),
                  xdram_ap(xdram, off, csize), writes=[B_XST[buf]])
        if buf == 0:
            xbf0 = xbf_off(0)
            xb_bufs = [B_XB[0]]
            sqb = 2 * (a2f(0) + 8 * M + 2 * 512)
            sq_bufs = [ubuf(0, 4), ubuf(0, 5)]
        else:
            xbf0 = sq_off(0, 0)
            xb_bufs = [B_SQ[0]]
            sqb = 2 * (a2f(0) + 8 * M + 3 * 512)
            sq_bufs = [ubuf(0, 6), ubuf(0, 7)]
        V(lambda e, xbf0=xbf0, buf=buf: e.tensor_copy(
            out=big2[:, xbf0:xbf0 + 8 * csize], in_=xst_ap(buf, csize)),
          reads=[B_XST[buf]], writes=xb_bufs, extra=[ldx])
        S(lambda e, sqb=sqb, buf=buf: e.activation(
            out=big2[:, sqb:sqb + 8 * csize], in_=xst_ap(buf, csize),
            func=AF.Square),
          reads=[B_XST[buf]], writes=sq_bufs, extra=[ldx])

        def mk_sums(e, xbf0=xbf0, sqb=sqb):
            last = None
            for b in range(8):
                last = e.matmul(psum[3][0:33, 0:csize], sel_a[:],
                                big2[:, xbf0 + b * csize:xbf0 + (b + 1) * csize],
                                start=(b == 0), stop=False)
            for b in range(8):
                last = e.matmul(psum[3][0:33, 0:csize], sel_b[:],
                                big2[:, sqb + b * csize:sqb + (b + 1) * csize],
                                start=False, stop=(b == 7))
            return last
        stat_pipeline(csize, D, xb_bufs + sq_bufs, mk_sums,
                      mu_out=mu_out, r_out=r_out, sq=0, par=buf)

    def sf(q, i, n=M):
        base = a2f(q) + 8 * M + i * 512
        return big2f[:, base:base + n]

    def si32(q, i, n=M):
        base = a2f(q) + 8 * M + i * 512
        return big2i[:, base:base + n]

    def sfb(q, i):
        return [ubuf(q, 2 * i), ubuf(q, 2 * i + 1)]

    def tl(i, n=M):
        return invt[:, (1 + i) * M:(1 + i) * M + n]

    def tlb(i):
        return B_INV[1 + i]

    def emit_sincos(q, P, A, B, C, sin_t, cos_t, tf=None, tfb=None):
        tf = tf or tl
        tfb = tfb or tlb
        V(lambda e: e.tensor_scalar(out=sf(q, A), in0=sf(q, P),
                                    scalar1=INV2PI, scalar2=0.5,
                                    op0=ALU.mult, op1=ALU.add),
          reads=sfb(q, P), writes=sfb(q, A))
        V(lambda e: e.tensor_copy(out=si32(q, C), in_=sf(q, A)),
          reads=sfb(q, A), writes=sfb(q, C))
        V(lambda e: e.tensor_copy(out=sf(q, A), in_=si32(q, C)),
          reads=sfb(q, C), writes=sfb(q, A))
        V(lambda e: e.scalar_tensor_tensor(out=sf(q, C), in0=sf(q, A),
                                           scalar=-CW1, in1=sf(q, P),
                                           op0=ALU.mult, op1=ALU.add),
          reads=sfb(q, A) + sfb(q, P), writes=sfb(q, C))
        V(lambda e: e.scalar_tensor_tensor(out=sf(q, B), in0=sf(q, A),
                                           scalar=-CW2, in1=sf(q, C),
                                           op0=ALU.mult, op1=ALU.add),
          reads=sfb(q, A) + sfb(q, C), writes=sfb(q, B))
        V(lambda e: e.tensor_scalar(out=sf(q, C), in0=sf(q, B), scalar1=-PI_F,
                                    scalar2=None, op0=ALU.is_lt),
          reads=sfb(q, B), writes=sfb(q, C))
        V(lambda e: e.scalar_tensor_tensor(out=sf(q, A), in0=sf(q, C),
                                           scalar=TWO_PI_F, in1=sf(q, B),
                                           op0=ALU.mult, op1=ALU.add),
          reads=sfb(q, C) + sfb(q, B), writes=sfb(q, A))
        S(lambda e: e.activation(out=tf(sin_t), in_=sf(q, A), func=AF.Sin),
          reads=sfb(q, A), writes=[tfb(sin_t)])
        V(lambda e: e.tensor_scalar(out=sf(q, C), in0=sf(q, A), scalar1=PI_F / 2,
                                    scalar2=None, op0=ALU.is_gt),
          reads=sfb(q, A), writes=sfb(q, C))
        V(lambda e: e.scalar_tensor_tensor(out=sf(q, B), in0=sf(q, C),
                                           scalar=-TWO_PI_F, in1=sf(q, A),
                                           op0=ALU.mult, op1=ALU.add),
          reads=sfb(q, C) + sfb(q, A), writes=sfb(q, B))
        S(lambda e: e.activation(out=tf(cos_t), in_=sf(q, B), func=AF.Sin,
                                 bias=epsc[:, 1:2], scale=1.0),
          reads=sfb(q, B) + [B_EPS], writes=[tfb(cos_t)])

    def xn_build(xdram, dram_base, mu_ap_fn, r_ap_fn, mu_bufs, r_bufs,
                 dst_fn=None, dst_bufs_fn=None):
        if dst_fn is None:
            dst_fn = lambda b, c: xn_hi[:, b * M + c * XC:b * M + (c + 1) * XC]
            dst_bufs_fn = lambda b, c: [B_XN]
        for c in range(M // XC):
            buf = c % 2
            ldx = dop("sync", f"LDS{buf}", xst_ap(buf, XC),
                      xdram_ap(xdram, dram_base + c * XC, XC), writes=[B_XST[buf]])
            for b in range(8):
                V(lambda e, b=b, buf=buf, c=c: e.tensor_tensor(
                    out=xst_blk(buf, b, XC), in0=xst_blk(buf, b, XC),
                    in1=mu_ap_fn(c), op=ALU.subtract),
                  reads=[B_XST[buf]] + mu_bufs, writes=[B_XST[buf]], extra=[ldx])
                V(lambda e, b=b, c=c, buf=buf: e.tensor_tensor(
                    out=dst_fn(b, c), in0=xst_blk(buf, b, XC),
                    in1=r_ap_fn(c), op=ALU.mult),
                  reads=[B_XST[buf]] + r_bufs, writes=dst_bufs_fn(b, c))

    BANKS6 = (0, 1, 2, 4, 5, 6)

    def emit_convert(dst, src, IN, lo, hi, done_buf, sem):
        for m in range(lo, hi):
            dop("gpsimd", sem,
                ap3(dst, m * 128 * IN, [[1, 1], [1, 128 * IN]]),
                ap3(src, m * 128 * IN, [[1, 1], [1, 128 * IN]]))
        done_buf.w = {sem: cnt[sem]}

    def stream_projection(wdram, IN, OUT, mov_fn, mov_bufs_fn, evac_fn,
                          extra=(), hook=None, reads_fn=None, split=None):
        K = IN // 128
        for m in range(OUT // 128):
            sbuf = m % 2
            dop("gpsimd", f"LDW{2 * sbuf}", wstage_ap(sbuf, K),
                wslice_ap(wdram, m, IN, OUT), writes=B_WS[sbuf],
                reads=(reads_fn(m) if reads_fn else ()))
            bank = BANKS6[m % 6]

            def mm(e, sbuf=sbuf, K=K, bank=bank, lo=0, hi=None):
                hi = K if hi is None else hi
                last = None
                for k in range(lo, hi):
                    wt_ap = wstage[:, sbuf * 4096 + k * 128:sbuf * 4096 + (k + 1) * 128]
                    last = e.matmul(psum[bank][:, 0:NMOV], wt_ap, mov_fn(k),
                                    start=(k == 0), stop=(k == K - 1))
                return last

            def mbufs(lo, hi):
                out = []
                for k in range(lo, hi):
                    for bb in mov_bufs_fn(k):
                        if bb not in out:
                            out.append(bb)
                return out
            if split is None:
                TE(mm, reads=B_WS[sbuf] + mbufs(0, K), writes=[B_PS[bank]],
                   extra=extra)
            else:
                TE(lambda e, mm=mm, split=split: mm(e, hi=split),
                   reads=B_WS[sbuf] + mbufs(0, split), writes=[B_PS[bank]],
                   extra=extra)
                TE(lambda e, mm=mm, split=split: mm(e, lo=split),
                   reads=B_WS[sbuf] + mbufs(split, K), writes=[B_PS[bank]],
                   extra=extra)
            evac_fn(m, bank)
            if hook is not None:
                hook()

    for c in range(T // XC):
        stats_chunk(dr["x_fm"], c * XC, XC, c % 2,
                    (lnbmu[:, c * XC:(c + 1) * XC], [B_LNM]),
                    (lnbr[:, c * XC:(c + 1) * XC], [B_LNR]))

    def pmu_ap(pp, c):
        b0 = a2f(1) + 8 * M + (2 * (pp % 2)) * 512
        return big2f[:, b0 + c * XC:b0 + (c + 1) * XC]

    def pr_ap(pp, c):
        b0 = a2f(1) + 8 * M + (2 * (pp % 2) + 1) * 512
        return big2f[:, b0 + c * XC:b0 + (c + 1) * XC]

    def pmu_bufs(pp):
        return sfb(1, 2 * (pp % 2))

    def pr_bufs(pp):
        return sfb(1, 2 * (pp % 2) + 1)

    def xnp_blk(pp, b):
        arena = pp // 2
        blk = 16 + 8 * (pp % 2) + b
        return a1b(arena) + blk * M

    def mov_xnp_fn(pp):
        def mov(k):
            base = xnp_blk(pp, k)
            return big1[:, base:base + M]
        return mov

    def mov_xnp_bufs_fn(pp):
        def bufs(k):
            arena = pp // 2
            return [B1REG[32 * arena + 16 + 8 * (pp % 2) + k]]
        return bufs

    def sigp_base(pp):
        return a2b(pp // 2) + (pp % 2) * 8 * M

    def emit_prev_stats(pp):
        for c in range(M // XC):
            buf = c % 2
            stats_chunk(dr["xp_fm"], pp * M + c * XC, XC, buf,
                        (pmu_ap(pp, c), pmu_bufs(pp)),
                        (pr_ap(pp, c), pr_bufs(pp)))
            for b in range(8):
                dst = big1[:, xnp_blk(pp, b) + c * XC:
                           xnp_blk(pp, b) + (c + 1) * XC]
                dbuf = mov_xnp_bufs_fn(pp)(b)
                V(lambda e, b=b, buf=buf, c=c, pp=pp: e.tensor_tensor(
                    out=xst_blk(buf, b, XC), in0=xst_blk(buf, b, XC),
                    in1=pmu_ap(pp, c), op=ALU.subtract),
                  reads=[B_XST[buf]] + pmu_bufs(pp), writes=[B_XST[buf]])
                V(lambda e, b=b, buf=buf, c=c, pp=pp, dst=dst: e.tensor_tensor(
                    out=dst, in0=xst_blk(buf, b, XC), in1=pr_ap(pp, c),
                    op=ALU.mult),
                  reads=[B_XST[buf]] + pr_bufs(pp), writes=dbuf)

    for pp in range(NP):
        emit_prev_stats(pp)

    def emit_wmag_prev(pp, hook=None):
        base = sigp_base(pp)

        def evac_sigp(m, bank, base=base):
            S(lambda e, m=m, bank=bank, base=base: e.activation(
                out=big2[:, base + m * M:base + (m + 1) * M],
                in_=psum[bank][:, 0:NMOV], func=AF.Sigmoid,
                bias=vec["v_bmag"][:, m:m + 1], scale=1.0),
              reads=[B_PS[bank], B_VEC["v_bmag"]],
              writes=b2r_abs(base + m * M, base + (m + 1) * M))
        stream_projection(dr["wmag_t"], D, D, mov_xnp_fn(pp),
                          mov_xnp_bufs_fn(pp), evac_sigp, hook=hook)

    def tlB(i, n=M):
        return big1[:, a1b(1) + i * M:a1b(1) + i * M + n]

    def tlbB(i):
        return B1REG[32 + i]

    def tlB0(i, n=M):
        return big1[:, i * M:i * M + n]

    def tlbB0(i):
        return B1REG[i]

    def sc_store(b, j, n=M):
        base = a1b(1) + (2 * b + j) * M
        return big1[:, base:base + n]

    def sc_store_b(b, j):
        return B1REG[32 + 2 * b + j]

    def emit_rot_scalars():
        def s8(i):
            return sf(0, i, 8)

        def s8b(i):
            return sfb(0, i)

        def s8i(i):
            return si32(0, i, 8)
        for jx in range(3):
            V(lambda e, jx=jx: e.tensor_scalar(
                out=s8(2), in0=vec["v_rp"][:], scalar1=0.01 * M * (jx + 1),
                scalar2=None, op0=ALU.mult),
              reads=[B_VEC["v_rp"]], writes=s8b(2))
            V(lambda e: e.tensor_scalar(out=s8(3), in0=s8(2), scalar1=INV2PI,
                                        scalar2=0.5, op0=ALU.mult, op1=ALU.add),
              reads=s8b(2), writes=s8b(3))
            V(lambda e: e.tensor_copy(out=s8i(5), in_=s8(3)),
              reads=s8b(3), writes=s8b(5))
            V(lambda e: e.tensor_copy(out=s8(3), in_=s8i(5)),
              reads=s8b(5), writes=s8b(3))
            V(lambda e: e.scalar_tensor_tensor(out=s8(5), in0=s8(3),
                                               scalar=-CW1, in1=s8(2),
                                               op0=ALU.mult, op1=ALU.add),
              reads=s8b(3) + s8b(2), writes=s8b(5))
            V(lambda e: e.scalar_tensor_tensor(out=s8(4), in0=s8(3),
                                               scalar=-CW2, in1=s8(5),
                                               op0=ALU.mult, op1=ALU.add),
              reads=s8b(3) + s8b(5), writes=s8b(4))
            V(lambda e: e.tensor_scalar(out=s8(5), in0=s8(4), scalar1=-PI_F,
                                        scalar2=None, op0=ALU.is_lt),
              reads=s8b(4), writes=s8b(5))
            V(lambda e: e.scalar_tensor_tensor(out=s8(3), in0=s8(5),
                                               scalar=TWO_PI_F, in1=s8(4),
                                               op0=ALU.mult, op1=ALU.add),
              reads=s8b(5) + s8b(4), writes=s8b(3))
            S(lambda e, jx=jx: e.activation(
                out=scf[:, 24 + 8 * jx:24 + 8 * jx + 8], in_=s8(3),
                func=AF.Sin),
              reads=s8b(3), writes=[B_SCF])
            V(lambda e: e.tensor_scalar(out=s8(5), in0=s8(3), scalar1=PI_F / 2,
                                        scalar2=None, op0=ALU.is_gt),
              reads=s8b(3), writes=s8b(5))
            V(lambda e: e.scalar_tensor_tensor(out=s8(4), in0=s8(5),
                                               scalar=-TWO_PI_F, in1=s8(3),
                                               op0=ALU.mult, op1=ALU.add),
              reads=s8b(5) + s8b(3), writes=s8b(4))
            S(lambda e, jx=jx: e.activation(
                out=scf[:, 8 * jx:8 * jx + 8], in_=s8(4), func=AF.Sin,
                bias=epsc[:, 1:2], scale=1.0),
              reads=s8b(4) + [B_EPS], writes=[B_SCF])

    def emit_lite(pp):
        uq = pp // 2
        base = sigp_base(pp)
        dop("sync", "LDP", xo[:, POS:POS + M],
            ap3(dr["pos_p"], pp * M, [[T, 128], [1, M]]), writes=[B_POS])

        def chain(b):
            if b % 2 == 0:
                return uq, tl, tlb
            return 1 - uq, tlB0, tlbB0

        def seg_a(b):
            sq, tf, tfb = chain(b)
            if pp == 0:
                V(lambda e, b=b, sq=sq: e.tensor_scalar(
                    out=sf(sq, 2), in0=xo[:, POS:POS + M],
                    scalar1=vec["v_rp"][:, b:b + 1],
                    scalar2=0.01, op0=ALU.mult, op1=ALU.mult),
                  reads=[B_POS, B_VEC["v_rp"]], writes=sfb(sq, 2))
                emit_sincos(sq, 2, 3, 4, 5, 0, 1,
                            tf=lambda i, b=b: sc_store(b, i),
                            tfb=lambda i, b=b: sc_store_b(b, i))
            else:
                jx = pp - 1
                cc = scf[:, 8 * jx + b:8 * jx + b + 1]
                sc = scf[:, 24 + 8 * jx + b:24 + 8 * jx + b + 1]
                V(lambda e, b=b, cc=cc: e.tensor_scalar(
                    out=tf(4), in0=sc_store(b, 0), scalar1=cc, scalar2=None,
                    op0=ALU.mult),
                  reads=[sc_store_b(b, 0), B_SCF], writes=[tfb(4)])
                V(lambda e, b=b, sc=sc: e.scalar_tensor_tensor(
                    out=tf(0), in0=sc_store(b, 1), scalar=sc, in1=tf(4),
                    op0=ALU.mult, op1=ALU.add),
                  reads=[sc_store_b(b, 1), B_SCF, tfb(4)], writes=[tfb(0)])
                V(lambda e, b=b, sc=sc: e.tensor_scalar(
                    out=tf(4), in0=sc_store(b, 0), scalar1=sc, scalar2=None,
                    op0=ALU.mult),
                  reads=[sc_store_b(b, 0), B_SCF, tfb(4)], writes=[tfb(4)])
                V(lambda e, b=b, cc=cc: e.scalar_tensor_tensor(
                    out=tf(1), in0=sc_store(b, 1), scalar=cc, in1=tf(4),
                    op0=ALU.mult, op1=ALU.subtract),
                  reads=[sc_store_b(b, 1), B_SCF, tfb(4)], writes=[tfb(1)])

        def seg_b(b):
            sq, tf, tfb = chain(b)
            if pp == 0:
                sin_ap, sin_b = sc_store(b, 0), sc_store_b(b, 0)
                cos_ap, cos_b = sc_store(b, 1), sc_store_b(b, 1)
            else:
                sin_ap, sin_b = tf(0), tfb(0)
                cos_ap, cos_b = tf(1), tfb(1)
            gsl = slice(base + b * M, base + (b + 1) * M)
            sreg = b2r_abs(gsl.start, gsl.stop)
            xnpb = xnp_blk(pp, b)
            xnp = big1[:, xnpb:xnpb + M]
            xnp_buf = mov_xnp_bufs_fn(pp)(b)
            V(lambda e, gsl=gsl, xnp=xnp: e.tensor_tensor(
                out=tf(2), in0=big2[:, gsl], in1=xnp, op=ALU.mult),
              reads=sreg + xnp_buf, writes=[tfb(2)])
            V(lambda e, cos_ap=cos_ap: e.tensor_tensor(
                out=tf(3), in0=tf(2), in1=cos_ap, op=ALU.mult),
              reads=[tfb(2), cos_b], writes=[tfb(3)])
            V(lambda e, sq=sq: e.tensor_scalar(
                out=tf(4), in0=tf(3), scalar1=1.0, scalar2=None,
                op0=ALU.mult, op1=ALU.add, accum_out=sf(sq, 3, 1)),
              reads=[tfb(3)], writes=[tfb(4)] + sfb(sq, 3))
            V(lambda e, b=b, sq=sq: e.tensor_tensor(
                out=carries[:, b:b + 1], in0=carries[:, b:b + 1],
                in1=sf(sq, 3, 1), op=ALU.add),
              reads=sfb(sq, 3) + [B_CR], writes=[B_CR])
            V(lambda e, sin_ap=sin_ap: e.tensor_tensor(
                out=tf(3), in0=tf(2), in1=sin_ap, op=ALU.mult),
              reads=[tfb(2), sin_b], writes=[tfb(3)])
            V(lambda e, sq=sq: e.tensor_scalar(
                out=tf(4), in0=tf(3), scalar1=1.0, scalar2=None,
                op0=ALU.mult, op1=ALU.add, accum_out=sf(sq, 3, 1)),
              reads=[tfb(3)], writes=[tfb(4)] + sfb(sq, 3))
            V(lambda e, b=b, sq=sq: e.tensor_tensor(
                out=carries[:, 8 + b:9 + b], in0=carries[:, 8 + b:9 + b],
                in1=sf(sq, 3, 1), op=ALU.add),
              reads=sfb(sq, 3) + [B_CI], writes=[B_CI])
            V(lambda e, gsl=gsl, sq=sq: e.tensor_scalar(
                out=tf(4), in0=big2[:, gsl], scalar1=1.0, scalar2=None,
                op0=ALU.mult, op1=ALU.add, accum_out=sf(sq, 3, 1)),
              reads=sreg + [tfb(4)], writes=[tfb(4)] + sfb(sq, 3))
            V(lambda e, b=b, sq=sq: e.tensor_tensor(
                out=carries[:, 16 + b:17 + b], in0=carries[:, 16 + b:17 + b],
                in1=sf(sq, 3, 1), op=ALU.add),
              reads=sfb(sq, 3) + [B_CM], writes=[B_CM])

        for bp in range(4):
            seg_a(2 * bp)
            seg_a(2 * bp + 1)
            seg_b(2 * bp)
            seg_b(2 * bp + 1)

    def emit_A0(pm):
        pbase = pm * M
        xn_build(dr["x_fm"], pbase,
                 lambda c, pbase=pbase: lnbmu[:, pbase + c * XC:pbase + (c + 1) * XC],
                 lambda c, pbase=pbase: lnbr[:, pbase + c * XC:pbase + (c + 1) * XC],
                 [B_LNM], [B_LNR])

    _h1ch = []
    for _m in range(32):
        def _c(_m=_m):
            dop("gpsimd", "CVT",
                ap3(dr["h1b"], _m * 128 * DH, [[1, 1], [1, 128 * DH]]),
                ap3(dr["h1_t"], _m * 128 * DH, [[1, 1], [1, 128 * DH]]))
        _h1ch.append(_c)
    _wg = [0]

    def _wmag_hook():
        _wg[0] += 1
        if _wg[0] % 3 == 0 and _h1ch:
            _h1ch.pop(0)()
    for pp in range(NP):
        emit_wmag_prev(pp, hook=_wmag_hook)
    while _h1ch:
        _h1ch.pop(0)()
    B_CVH1.w = {"CVT": cnt["CVT"]}
    emit_A0(0)
    emit_rot_scalars()
    for pp in range(NP):
        emit_lite(pp)

    V(lambda e: e.tensor_scalar(out=carries[:, 0:24], in0=carries[:, 0:24],
                                scalar1=mask_sb[:, 0:1], scalar2=None, op0=ALU.mult),
      reads=[B_MASK, B_CR, B_CI, B_CM], writes=[B_CR, B_CI, B_CM])
    V(lambda e: e.tensor_scalar(out=carries[:, 16:24], in0=carries[:, 16:24],
                                scalar1=EPS_ACC, scalar2=None, op0=ALU.add),
      reads=[B_CM], writes=[B_CM])

    def mov_xn(k):
        return xn_hi[:, k * M:(k + 1) * M]

    def emit_A1(pm):
        q = pm % 2
        KD = D // 128
        for proj, wdram_, obase, evbias, evfunc in [
                (0, dr["wmag_t"], a2b(q), "v_bmag", AF.Sigmoid),
                (1, dr["wq_t"], a2b(q) + 8 * M, "v_bq", AF.Identity)]:
            for g4 in range(KD // 4):
                h = g4 % 2
                ap_w = ap3(wstage, h * 4096, [[8192, 128], [1, 4096]])
                dop("gpsimd", f"LDW{2 * h}", ap_w,
                    wslice_ap(wdram_, 4 * g4, D, D, ngroups=4),
                    writes=B_WS[h])
                for mi in range(4):
                    m = 4 * g4 + mi
                    bank = (m % 3) if proj == 0 else 4 + (m % 3)

                    def mm1(e, h=h, mi=mi, bank=bank, KD=KD):
                        last = None
                        for k in range(KD):
                            wt_ap = wstage[:, h * 4096 + mi * 1024 + k * 128:
                                           h * 4096 + mi * 1024 + (k + 1) * 128]
                            last = e.matmul(psum[bank][:, 0:NMOV], wt_ap,
                                            mov_xn(k), start=(k == 0),
                                            stop=(k == KD - 1))
                        return last
                    TE(mm1, reads=B_WS[h] + [B_XN], writes=[B_PS[bank]])
                    S(lambda e, m=m, bank=bank, obase=obase, evbias=evbias,
                      evfunc=evfunc: e.activation(
                        out=big2[:, obase + m * M:obase + (m + 1) * M],
                        in_=psum[bank][:, 0:NMOV], func=evfunc,
                        bias=vec[evbias][:, m:m + 1], scale=1.0),
                      reads=[B_PS[bank], B_VEC[evbias]],
                      writes=b2r_abs(obase + m * M, obase + (m + 1) * M))

    def emit_mid(pm, alt=False):
        q = pm % 2
        sigb = a2b(q)
        qofb = a2b(q) + 8 * M
        ctxb = a1b(q)
        per_b = []

        def seg_pos(pm=pm):
            dop("sync", "LDP", xo[:, POS:POS + M],
                ap3(dr["pos_o"], pm * M, [[T, 128], [1, M]]), writes=[B_POS])
        for b in range(8):
            if alt and b % 2 == 1:
                sq = 1 - q
                tf, tfb = tlB, tlbB
                inv_ap, inv_rb = tlB(7), tlbB(7)
            else:
                sq = q
                tf, tfb = tl, tlb
                inv_ap, inv_rb = invt[:, 0:M], B_INV[0]
            bsl = slice(b * M, (b + 1) * M)
            gs = slice(sigb + bsl.start, sigb + bsl.stop)
            gq = slice(qofb + bsl.start, qofb + bsl.stop)
            sreg = b2r_abs(gs.start, gs.stop)
            qreg = b2r_abs(gq.start, gq.stop)
            xcr = b1r(ctxb + bsl.start, ctxb + bsl.stop)
            xsr = b1r(ctxb + 8 * M + bsl.start, ctxb + 8 * M + bsl.stop)
            rrr = b1r(ctxb + 16 * M + bsl.start, ctxb + 16 * M + bsl.stop)
            rir = b1r(ctxb + 24 * M + bsl.start, ctxb + 24 * M + bsl.stop)

            def seg1(b=b, sq=sq, gs=gs, sreg=sreg, inv_ap=inv_ap,
                     inv_rb=inv_rb):
                V(lambda e, gs=gs, b=b, sq=sq: e.tensor_tensor_scan(
                    out=sf(sq, 0), data0=big2[:, gs], data1=big2[:, gs],
                    initial=carries[:, 16 + b:17 + b], op0=ALU.add,
                    op1=ALU.bypass),
                  reads=sreg + [B_CM], writes=sfb(sq, 0))
                V(lambda e, b=b, sq=sq: e.tensor_copy(
                    out=carries[:, 16 + b:17 + b], in_=sf(sq, 0)[:, M - 1:M]),
                  reads=sfb(sq, 0), writes=[B_CM])
                S(lambda e, sq=sq: e.activation(out=sf(sq, 1), in_=sf(sq, 0),
                                                func=AF.Ln),
                  reads=sfb(sq, 0), writes=sfb(sq, 1))
                S(lambda e, sq=sq, inv_ap=inv_ap: e.activation(
                    out=inv_ap, in_=sf(sq, 1), func=AF.Exp, scale=-0.5),
                  reads=sfb(sq, 1), writes=[inv_rb])

            def seg2(b=b, sq=sq, tf=tf, tfb=tfb):
                V(lambda e, b=b, sq=sq: e.tensor_scalar(
                    out=sf(sq, 2), in0=xo[:, POS:POS + M],
                    scalar1=vec["v_rp"][:, b:b + 1],
                    scalar2=0.01, op0=ALU.mult, op1=ALU.mult),
                  reads=[B_POS, B_VEC["v_rp"]], writes=sfb(sq, 2))
                emit_sincos(sq, 2, 3, 4, 5, 0, 1, tf=tf, tfb=tfb)

            def seg3(sq=sq, gq=gq, qreg=qreg, tf=tf, tfb=tfb):
                V(lambda e, gq=gq, sq=sq: e.tensor_tensor(
                    out=sf(sq, 0), in0=sf(sq, 2), in1=big2[:, gq], op=ALU.add),
                  reads=sfb(sq, 2) + qreg, writes=sfb(sq, 0))
                emit_sincos(sq, 0, 3, 4, 5, 2, 3, tf=tf, tfb=tfb)

            def seg4(gs=gs, bsl=bsl, sreg=sreg, ctxb=ctxb, xcr=xcr, xsr=xsr,
                     tf=tf, tfb=tfb):
                V(lambda e, gs=gs, bsl=bsl: e.tensor_tensor(
                    out=tf(4), in0=big2[:, gs], in1=xn_hi[:, bsl], op=ALU.mult),
                  reads=sreg + [B_XN], writes=[tfb(4)])
                V(lambda e, bsl=bsl, ctxb=ctxb: e.tensor_tensor(
                    out=big1[:, ctxb + bsl.start:ctxb + bsl.stop],
                    in0=xn_hi[:, bsl], in1=tf(1), op=ALU.mult),
                  reads=[B_XN, tfb(1)], writes=xcr)
                V(lambda e, bsl=bsl, ctxb=ctxb: e.tensor_tensor(
                    out=big1[:, ctxb + 8 * M + bsl.start:
                              ctxb + 8 * M + bsl.stop],
                    in0=xn_hi[:, bsl], in1=tf(0), op=ALU.mult),
                  reads=[B_XN, tfb(0)], writes=xsr)
                V(lambda e: e.tensor_tensor(out=tf(5), in0=tf(4), in1=tf(1),
                                            op=ALU.mult),
                  reads=[tfb(4), tfb(1)], writes=[tfb(5)])
                V(lambda e: e.tensor_tensor(out=tf(6), in0=tf(4), in1=tf(0),
                                            op=ALU.mult),
                  reads=[tfb(4), tfb(0)], writes=[tfb(6)])

            def seg5(b=b, sq=sq, tf=tf, tfb=tfb):
                V(lambda e, b=b, sq=sq: e.tensor_tensor_scan(
                    out=sf(sq, 3), data0=tf(5), data1=tf(5),
                    initial=carries[:, b:b + 1], op0=ALU.add, op1=ALU.bypass),
                  reads=[tfb(5), B_CR], writes=sfb(sq, 3))
                V(lambda e, b=b, sq=sq: e.tensor_copy(
                    out=carries[:, b:b + 1], in_=sf(sq, 3)[:, M - 1:M]),
                  reads=sfb(sq, 3), writes=[B_CR])
                V(lambda e, b=b, sq=sq: e.tensor_tensor_scan(
                    out=sf(sq, 4), data0=tf(6), data1=tf(6),
                    initial=carries[:, 8 + b:9 + b], op0=ALU.add,
                    op1=ALU.bypass),
                  reads=[tfb(6), B_CI], writes=sfb(sq, 4))
                V(lambda e, b=b, sq=sq: e.tensor_copy(
                    out=carries[:, 8 + b:9 + b], in_=sf(sq, 4)[:, M - 1:M]),
                  reads=sfb(sq, 4), writes=[B_CI])

            def seg6(sq=sq, bsl=bsl, ctxb=ctxb, rrr=rrr, rir=rir, tf=tf,
                     tfb=tfb, inv_ap=inv_ap, inv_rb=inv_rb):
                V(lambda e, sq=sq, inv_ap=inv_ap: e.scalar_tensor_tensor(
                    out=tf(4), in0=sf(sq, 3), scalar=SQRT5, in1=inv_ap,
                    op0=ALU.mult, op1=ALU.mult),
                  reads=sfb(sq, 3) + [inv_rb], writes=[tfb(4)])
                V(lambda e, sq=sq, inv_ap=inv_ap: e.scalar_tensor_tensor(
                    out=tf(5), in0=sf(sq, 4), scalar=SQRT5, in1=inv_ap,
                    op0=ALU.mult, op1=ALU.mult),
                  reads=sfb(sq, 4) + [inv_rb], writes=[tfb(5)])
                V(lambda e: e.tensor_tensor(out=tf(6), in0=tf(4), in1=tf(3),
                                            op=ALU.mult),
                  reads=[tfb(4), tfb(3)], writes=[tfb(6)])
                V(lambda e: e.tensor_tensor(out=tf(0), in0=tf(5), in1=tf(2),
                                            op=ALU.mult),
                  reads=[tfb(5), tfb(2)], writes=[tfb(0)])
                V(lambda e, bsl=bsl, ctxb=ctxb: e.tensor_tensor(
                    out=big1[:, ctxb + 16 * M + bsl.start:
                              ctxb + 16 * M + bsl.stop],
                    in0=tf(6), in1=tf(0), op=ALU.add),
                  reads=[tfb(6), tfb(0)], writes=rrr)
                V(lambda e: e.tensor_tensor(out=tf(1), in0=tf(5), in1=tf(3),
                                            op=ALU.mult),
                  reads=[tfb(5), tfb(3)], writes=[tfb(1)])
                V(lambda e: e.tensor_tensor(out=tf(2), in0=tf(4), in1=tf(2),
                                            op=ALU.mult),
                  reads=[tfb(4), tfb(2)], writes=[tfb(2)])
                V(lambda e, bsl=bsl, ctxb=ctxb: e.tensor_tensor(
                    out=big1[:, ctxb + 24 * M + bsl.start:
                              ctxb + 24 * M + bsl.stop],
                    in0=tf(1), in1=tf(2), op=ALU.subtract),
                  reads=[tfb(1), tfb(2)], writes=rir)
            per_b.append([seg1, seg2, seg3, seg4, seg5, seg6])
        segs = [seg_pos]
        if alt:
            for bp in range(4):
                for si in range(6):
                    segs.append(per_b[2 * bp][si])
                    segs.append(per_b[2 * bp + 1][si])
        else:
            for b in range(8):
                segs += per_b[b]
        return segs

    def emit_ln2(pm):
        q = pm % 2
        ctxb = a1b(q)
        nacc = 0
        sqi = 0
        B_SQS = [Buf(f"sqs{pm}_0"), Buf(f"sqs{pm}_1")]
        for b in range(8):
            for g in range(4):
                cb = g * 8 + b
                gsb = ctxb + cb * M
                breg = b1r(gsb, gsb + M)
                first = nacc == 0
                last = nacc == 62
                nacc += 2
                TE(lambda e, gsb=gsb, first=first: e.matmul(
                    psum[3][0:33, 0:NMOV], sel_a[:], big1[:, gsb:gsb + M],
                    start=first, stop=False),
                   reads=breg + [B_SELA], writes=[B_PS[3]])
                slot = sqi % 2
                sqi += 1
                S(lambda e, gsb=gsb, slot=slot, q=q: e.activation(
                    out=big2[:, sq_off(q, slot):sq_off(q, slot) + NMOV],
                    in_=big1[:, gsb:gsb + M], func=AF.Square),
                  reads=breg, writes=[B_SQS[slot], B_SQ[q]])
                TE(lambda e, slot=slot, last=last, q=q: e.matmul(
                    psum[3][0:33, 0:NMOV], sel_b[:],
                    big2[:, sq_off(q, slot):sq_off(q, slot) + NMOV],
                    start=False, stop=last),
                   reads=[B_SQS[slot], B_SELB, B_SQ[q]], writes=[B_PS[3]])
        r2_ap = big2f[:, a2f(q) + 8 * M + 4 * MIDC:a2f(q) + 8 * M + 4 * MIDC + M]
        v_ap = big2f[:, a2f(q) + 8 * M + 6 * MIDC:a2f(q) + 8 * M + 6 * MIDC + M]
        stat_pipeline(M, DH, [], None,
                      r_out=(r2_ap, [ubuf(q, 4), ubuf(q, 5)]),
                      v_out=(v_ap, [ubuf(q, 6), ubuf(q, 7)]), sq=q)
        S(lambda e, q=q, r2_ap=r2_ap: e.activation(
            out=big2[:, sq_off(q, 0):sq_off(q, 0) + M], in_=r2_ap, func=AF.Copy),
          reads=[ubuf(q, 4), ubuf(q, 5)], writes=[B_SQ[q]])

    def emit_a3b_block(pm, b):
        q = pm % 2
        ctxb = a1b(q)
        gsb = ctxb + b * M
        breg = b1r(gsb, gsb + M)
        V(lambda e, gsb=gsb, q=q: e.tensor_tensor(
            out=big1[:, gsb:gsb + M], in0=big1[:, gsb:gsb + M],
            in1=big2[:, sq_off(q, 0):sq_off(q, 0) + M], op=ALU.mult),
          reads=breg + [B_SQ[q]], writes=breg)
        v_ap = big2f[:, a2f(q) + 8 * M + 6 * MIDC:a2f(q) + 8 * M + 6 * MIDC + M]
        V(lambda e, gsb=gsb, q=q, v_ap=v_ap: e.tensor_tensor(
            out=uf_bf(q, 0, M), in0=big1[:, gsb:gsb + M],
            in1=v_ap, op=ALU.subtract),
          reads=breg + [ubuf(q, 6), ubuf(q, 7)], writes=[ubuf(q, 0)])
        V(lambda e, b=b, gsb=gsb, q=q: e.tensor_scalar(
            out=big1[:, gsb:gsb + M], in0=uf_bf(q, 0, M),
            scalar1=vec["v_lng"][:, b:b + 1], scalar2=vec["v_lnb"][:, b:b + 1],
            op0=ALU.mult, op1=ALU.add),
          reads=[ubuf(q, 0), B_VEC["v_lng"], B_VEC["v_lnb"]], writes=breg)

    emit_A1(0)
    emit_convert(dr["h2b"], dr["h2_t"], DH, 0, 32, B_CVH2, "CV2")
    for _s in emit_mid(0, alt=True):
        _s()
    emit_ln2(0)
    for _b in range(32):
        emit_a3b_block(0, _b)

    for pm in range(NP):
        q = pm % 2
        ctxb = a1b(q)
        yb = a2b(q)

        def mov_ctx(k, ctxb=ctxb):
            return big1[:, ctxb + k * M:ctxb + (k + 1) * M]

        def mov_ctx_bufs(k, ctxb=ctxb):
            return b1r(ctxb + k * M, ctxb + (k + 1) * M)

        def evac_a1(m, bank, yb=yb):
            S(lambda e, m=m, bank=bank, yb=yb: e.activation(
                out=big2[:, yb + m * M:yb + (m + 1) * M],
                in_=psum[bank][:, 0:NMOV], func=AF.Gelu,
                bias=vec["v_b1"][:, m:m + 1], scale=1.0),
              reads=[B_PS[bank], B_VEC["v_b1"]],
              writes=b2r_abs(yb + m * M, yb + (m + 1) * M))
        stream_projection(dr["w1_t"], DH, DA, mov_ctx, mov_ctx_bufs, evac_a1,
                          split=16)

        def mov_a1(k, yb=yb):
            return big2[:, yb + k * M:yb + (k + 1) * M]

        def mov_a1_bufs(k, yb=yb):
            return b2r_abs(yb + k * M, yb + (k + 1) * M)

        def evac_o(m, bank):
            S(lambda e, m=m, bank=bank: e.activation(
                out=obuf[:, m * M:(m + 1) * M],
                in_=psum[bank][:, 0:NMOV], func=AF.Identity,
                bias=vec["v_b2"][:, m:m + 1], scale=1.0),
              reads=[B_PS[bank], B_VEC["v_b2"]],
              writes=obr(m * M, (m + 1) * M))
        stream_projection(dr["w2_t"], DA, D, mov_a1, mov_a1_bufs, evac_o)

        KD = D // 128
        for m in range(DH // 128):
            p2 = m % 2
            slotA, slotB = p2, 2 + p2
            bankA = m % 3
            bankB = 4 + m % 3
            apA = ap3(wstage, slotA * 2048, [[8192, 128], [1, 1024]])
            apB = ap3(wstage, slotB * 2048, [[8192, 128], [1, 1024]])
            dop("gpsimd", f"LDW{slotA}", apA, wslice_ap(dr["fw1_t"], m, D, DH),
                writes=[B_WS4[slotA]])
            dop("gpsimd", f"LDW{slotB}", apB, wslice_ap(dr["fw2_t"], m, D, DH),
                writes=[B_WS4[slotB]])

            def mmA(e, slotA=slotA, bankA=bankA):
                last = None
                for k in range(KD):
                    wt_ap = wstage[:, slotA * 2048 + k * 128:slotA * 2048 + (k + 1) * 128]
                    last = e.matmul(psum[bankA][:, 0:NMOV], wt_ap, mov_xn(k),
                                    start=(k == 0), stop=(k == KD - 1))
                return last
            TE(mmA, reads=[B_WS4[slotA], B_XN], writes=[B_PS[bankA]])

            def mmB(e, slotB=slotB, bankB=bankB):
                last = None
                for k in range(KD):
                    wt_ap = wstage[:, slotB * 2048 + k * 128:slotB * 2048 + (k + 1) * 128]
                    last = e.matmul(psum[bankB][:, 0:NMOV], wt_ap, mov_xn(k),
                                    start=(k == 0), stop=(k == KD - 1))
                return last
            TE(mmB, reads=[B_WS4[slotB], B_XN], writes=[B_PS[bankB]])
            silu_reg = b1r(ctxb, ctxb + M)
            S(lambda e, m=m, bankA=bankA, ctxb=ctxb: e.activation(
                out=big1f[:, (ctxb // 2):(ctxb // 2) + M],
                in_=psum[bankA][:, 0:NMOV], func=AF.Silu,
                scale=vec["v_s1"][:, m:m + 1]),
              reads=[B_PS[bankA], B_VEC["v_s1"]], writes=silu_reg)
            V(lambda e, m=m, bankB=bankB, yb=yb, ctxb=ctxb: e.scalar_tensor_tensor(
                out=big2[:, yb + m * M:yb + (m + 1) * M],
                in0=psum[bankB][:, 0:NMOV], scalar=vec["v_s2"][:, m:m + 1],
                in1=big1f[:, (ctxb // 2):(ctxb // 2) + M],
                op0=ALU.mult, op1=ALU.mult),
              reads=[B_PS[bankB], B_VEC["v_s2"]] + silu_reg,
              writes=b2r_abs(yb + m * M, yb + (m + 1) * M))

        if pm + 1 < NP:
            emit_A0(pm + 1)
            emit_A1(pm + 1)
            pending = emit_mid(pm + 1)
        else:
            pending = []

        def hook(pending=pending):
            if pending:
                pending.pop(0)()

        def mov_f(k, yb=yb):
            return big2[:, yb + k * M:yb + (k + 1) * M]

        def mov_f_bufs(k, yb=yb):
            return b2r_abs(yb + k * M, yb + (k + 1) * M)

        def evac_g1(m, bank, ctxb=ctxb):
            S(lambda e, m=m, bank=bank, ctxb=ctxb: e.activation(
                out=big1[:, ctxb + m * M:ctxb + (m + 1) * M],
                in_=psum[bank][:, 0:NMOV], func=AF.Gelu,
                scale=vec["v_h1s"][:, m:m + 1]),
              reads=[B_PS[bank], B_VEC["v_h1s"]],
              writes=b1r(ctxb + m * M, ctxb + (m + 1) * M))
        stream_projection(dr["h1b"], DH, DH, mov_f, mov_f_bufs, evac_g1,
                          hook=hook, reads_fn=lambda m: [B_CVH1])

        def mov_g1(k, ctxb=ctxb):
            return big1[:, ctxb + k * M:ctxb + (k + 1) * M]

        def mov_g1_bufs(k, ctxb=ctxb):
            return b1r(ctxb + k * M, ctxb + (k + 1) * M)

        def evac_g2(m, bank, yb=yb):
            S(lambda e, m=m, bank=bank, yb=yb: e.activation(
                out=big2[:, yb + m * M:yb + (m + 1) * M],
                in_=psum[bank][:, 0:NMOV], func=AF.Gelu,
                scale=vec["v_h2s"][:, m:m + 1]),
              reads=[B_PS[bank], B_VEC["v_h2s"]],
              writes=b2r_abs(yb + m * M, yb + (m + 1) * M))
        stream_projection(dr["h2b"], DH, DH, mov_g1, mov_g1_bufs, evac_g2,
                          hook=hook, reads_fn=lambda m: [B_CVH2])

        while pending:
            pending.pop(0)()
        if pm + 1 < NP:
            emit_ln2(pm + 1)

        def mov_g2(k, yb=yb):
            return big2[:, yb + k * M:yb + (k + 1) * M]

        def mov_g2_bufs(k, yb=yb):
            return b2r_abs(yb + k * M, yb + (k + 1) * M)

        def evac_w3(m, bank, pm=pm):
            oreg = obr(m * M, (m + 1) * M)
            V(lambda e, m=m, bank=bank: e.scalar_tensor_tensor(
                out=obuf[:, m * M:(m + 1) * M],
                in0=psum[bank][:, 0:NMOV], scalar=vec["v_s3"][:, m:m + 1],
                in1=obuf[:, m * M:(m + 1) * M],
                op0=ALU.mult, op1=ALU.add),
              reads=[B_PS[bank], B_VEC["v_s3"]] + oreg, writes=oreg)
            if pm + 1 < NP:
                for j in range(4):
                    emit_a3b_block(pm + 1, 4 * m + j)
        stream_projection(dr["w3_t"], DH, D, mov_g2, mov_g2_bufs, evac_w3)

        pbase = pm * M

        def ob_f(buf):
            return a2f(q) + 8 * M + buf * 1024

        def ob_bufs(buf):
            return [ubuf(q, 4 * buf + j) for j in range(4)]
        for c in range(M // XC):
            buf = c % 2
            obase = ob_f(buf)
            ldx = dop("sync", f"LDS{buf}", xst_ap(buf, XC),
                      xdram_ap(dr["x_fm"], pbase + c * XC, XC), writes=[B_XST[buf]])
            for b in range(8):
                oreg = obr(b * M + c * XC, b * M + (c + 1) * XC)
                V(lambda e, b=b, buf=buf, c=c, pbase=pbase, q=q: e.tensor_tensor(
                    out=uf(q, 8, XC), in0=xst_blk(buf, b, XC),
                    in1=lnbmu[:, pbase + c * XC:pbase + (c + 1) * XC],
                    op=ALU.subtract),
                  reads=[B_XST[buf], B_LNM], writes=[ubuf(q, 8)], extra=[ldx])
                V(lambda e, c=c, pbase=pbase, q=q: e.tensor_tensor(
                    out=uf(q, 9, XC), in0=uf(q, 8, XC),
                    in1=lnbr[:, pbase + c * XC:pbase + (c + 1) * XC],
                    op=ALU.mult),
                  reads=[ubuf(q, 8), B_LNR], writes=[ubuf(q, 9)])
                V(lambda e, b=b, buf=buf, q=q: e.tensor_tensor(
                    out=uf(q, 8, XC), in0=uf(q, 9, XC), in1=xst_blk(buf, b, XC),
                    op=ALU.add),
                  reads=[ubuf(q, 9), B_XST[buf]], writes=[ubuf(q, 8)])
                V(lambda e, b=b, c=c, obase=obase, q=q: e.tensor_tensor(
                    out=big2f[:, obase + b * XC:obase + (b + 1) * XC],
                    in0=uf(q, 8, XC),
                    in1=obuf[:, b * M + c * XC:b * M + (c + 1) * XC], op=ALU.add),
                  reads=[ubuf(q, 8)] + oreg, writes=ob_bufs(buf))
            gc = (pbase + c * XC) // XC
            dop("sync", f"STS{buf}",
                ap3(dr["y_fm"], gc * (128 * 8 * XC), [[8 * XC, 128], [1, 8 * XC]]),
                ap3(big2f, ob_f(buf), [[32 * M, 128], [1, 8 * XC]]),
                reads=ob_bufs(buf))

    t0_, t1_ = cnt["STS0"], cnt["STS1"]
    q_["sync"].append(lambda e: e.wait_ge(sems["STS0"], t0_))
    q_["sync"].append(lambda e: e.wait_ge(sems["STS1"], t1_))

    block = es.enter_context(nc.Block())

    @block.sync
    def _(e):
        for f in q_["sync"]:
            f(e)

    @block.gpsimd
    def _(e):
        for f in q_["gpsimd"]:
            f(e)

    @block.vector
    def _(e):
        for f in q_["vector"]:
            f(e)

    @block.scalar
    def _(e):
        for f in q_["scalar"]:
            f(e)

    @block.tensor
    def _(e):
        for f in q_["tensor"]:
            f(e)

    es.close()
    return nc


def _layout_inputs(ins, T=2048, XC=128):
    f32 = np.float32
    x = np.asarray(ins["L0_attn_x"], dtype=f32)
    B, Sq, Dm = x.shape
    H = Sq // 2

    def pack_w(a):
        wt_ = np.asarray(a, f32).T
        IN, OUT = wt_.shape
        return np.ascontiguousarray(
            wt_.reshape(IN // 128, 128, OUT // 128, 128).transpose(2, 1, 0, 3)
        ).reshape(OUT, IN)

    wt = {
        "wmag_t": pack_w(ins["L0_attn_W_mag"]),
        "wq_t": pack_w(ins["L0_attn_W_q"]),
        "w1_t": pack_w(ins["L0_attn_W1"]),
        "w2_t": pack_w(ins["L0_attn_W2"]),
        "fw1_t": pack_w(ins["L0_ffn_w1_w"]),
        "fw2_t": pack_w(ins["L0_ffn_w2_w"]),
        "h1_t": pack_w(ins["L0_ffn_h1_w"]),
        "h2_t": pack_w(ins["L0_ffn_h2_w"]),
        "w3_t": pack_w(ins["L0_ffn_w3_w"]),
    }

    def v128(name, w):
        return np.ascontiguousarray(np.asarray(ins[name], f32).reshape(w, 128).T)

    vecs = {
        "v_bmag": v128("L0_attn_b_mag", 8), "v_bq": v128("L0_attn_b_q", 8),
        "v_b2": v128("L0_attn_b2", 8), "v_rp": v128("L0_attn_random_phases", 8),
        "v_s3": v128("L0_ffn_w3_s", 8), "v_b1": v128("L0_attn_b1", 16),
        "v_lng": v128("L0_attn_ln_g", 32), "v_lnb": v128("L0_attn_ln_b", 32),
        "v_s1": v128("L0_ffn_w1_s", 32), "v_s2": v128("L0_ffn_w2_s", 32),
        "v_h1s": v128("L0_ffn_h1_s", 32), "v_h2s": v128("L0_ffn_h2_s", 32),
    }
    ones2 = np.ones((33, 128), dtype=np.float32)
    sel_a = np.zeros((128, 33), dtype=ml_dtypes.bfloat16)
    sel_a[:, 0] = 1
    sel_b = np.zeros((128, 33), dtype=ml_dtypes.bfloat16)
    sel_b[:, 32] = 1
    pos = np.arange(Sq, dtype=f32)

    in_maps = []
    for core in range(8):
        b, h = core // 2, core % 2

        def pack_x(xt):
            return np.ascontiguousarray(
                xt.reshape(8, 128, H // XC, XC).transpose(2, 1, 0, 3)
            ).reshape(H // XC, 128 * 8 * XC)

        m = {
            "x_fm": pack_x(np.ascontiguousarray(x[b, h * H:(h + 1) * H].T)),
            "xp_fm": pack_x(np.ascontiguousarray(x[b, 0:H].T)),
            "pos_o": np.ascontiguousarray(
                np.broadcast_to(pos[h * H:(h + 1) * H], (128, H))),
            "pos_p": np.ascontiguousarray(np.broadcast_to(pos[0:H], (128, H))),
            "mask": np.full((128, 1), 1.0 if h == 1 else 0.0, dtype=f32),
            "ones2": ones2, "sel_a": sel_a, "sel_b": sel_b,
        }
        m.update(wt)
        m.update(vecs)
        in_maps.append(m)
    return in_maps


def kernel(**ins):
    in_maps = _layout_inputs(ins)
    nc = build_program(CFG())
    res = run_bass_kernel_spmd(nc, in_maps, core_ids=list(range(8)))
    x = np.asarray(ins["L0_attn_x"])
    B, Sq, Dm = x.shape
    H = Sq // 2
    out = np.empty((B, Sq, Dm), dtype=np.float32)
    XC = 128
    for core in range(8):
        b, h = core // 2, core % 2
        y4 = res.results[core]["y_fm"].reshape(H // XC, 128, 8, XC)
        y_fm = y4.transpose(2, 1, 0, 3).reshape(Dm, H)
        out[b, h * H:(h + 1) * H] = y_fm.T
    return out
